# version 10
# baseline (speedup 1.0000x reference)
import numpy as np
import ml_dtypes
from contextlib import ExitStack

import concourse.bass as bass
import concourse.mybir as mybir
from concourse.bass_core_rust import IndirectOffsetOnAxis
from concourse.bass_utils import run_bass_kernel_spmd

F32 = mybir.dt.float32
BF16 = mybir.dt.bfloat16
I32 = mybir.dt.int32
AF = mybir.ActivationFunctionType
ALU = mybir.AluOpType
AXX = mybir.AxisListType.X

NCORE = 8
T = 2048
NT = 16
E = 1024
KC = 8
H = 16
D = 64
CH = 128
G = NCORE * T
CAP = 2560
CAP1 = CAP + 1
EPS_LN = 1e-5
EPS_RMS = float(np.finfo(np.float32).eps)
BIG = 1024.0
IS = 8192
IR = 4096
SLAB = 512


def _slopes():
    def po2(m):
        start = 2.0 ** (-(2.0 ** (-(np.log2(m) - 3))))
        return [start * start ** i for i in range(m)]
    s = np.array(po2(H), dtype=np.float32)
    return (s * np.float32(1.0 - 0.0 / 11.0 + 1e-5)).astype(np.float32)


def _consts(core):
    s = _slopes()
    idx = np.arange(CH, dtype=np.float32)
    maskT = np.zeros((CH, H * CH), np.float32)
    for h in range(H):
        d = idx[:, None] - idx[None, :]
        m = np.where(d >= 0, np.exp(-s[h] * np.maximum(d, 0.0)), 0.0).astype(np.float32)
        maskT[:, h * CH:(h + 1) * CH] = m.T
    lam_q = np.exp(-s[:, None] * (idx[None, :] + 1.0)).astype(np.float32)
    lam_k = np.exp(-s[:, None] * (CH - 1.0 - idx[None, :])).astype(np.float32)
    lamqF = np.zeros((E, CH), np.float32)
    for r in range(E):
        lamqF[r] = lam_q[r // D]
    lamkT = np.zeros((CH, E), np.float32)
    for f in range(E):
        lamkT[:, f] = lam_k[f // D]
    pfx = (np.arange(128)[:, None] < np.arange(128)[None, :]).astype(np.float32)
    iden = np.eye(128, dtype=np.float32)
    iota8 = np.tile(np.arange(8, dtype=np.float32)[None, :], (128, 1))
    bmi = BIG - iota8
    mye = np.full((128, 1), float(core), np.float32)
    prefm = np.zeros((8, 128), np.float32)
    prefm[:core, :] = 1.0
    idsP = (np.arange(128, dtype=np.float32)[:, None] * 128.0
            + np.arange(128, dtype=np.float32)[None, :])
    ones_row = np.ones((1, 128), np.float32)
    ones_col = np.ones((128, 1), np.float32)
    iota640 = np.ascontiguousarray(np.broadcast_to(
        np.arange(640, dtype=np.float32)[None, :], (128, 640)))
    return dict(C_maskT=maskT, C_lamqF=lamqF, C_lamkT=lamkT, C_pfx=pfx,
                C_id=iden, C_iota8=iota8, C_bmi=bmi, C_mye=mye,
                C_prefm=prefm, C_idsP=idsP, C_onesr=ones_row, C_onesc=ones_col,
                C_iota640=iota640)


def build_nc(upto=None):
    nc = bass.Bass("TRN2", target_bir_lowering=False, debug=False,
                   detect_race_conditions=False)
    ap = {}

    def din(name, shape, dt=F32):
        ap[name] = nc.dram_tensor(name, shape, dt, kind="ExternalInput").ap()

    din("X", [T, E])
    din("Wq", [E, E]); din("Wk", [E, E]); din("Wv", [E, E]); din("Wo", [E, E])
    din("Wg1", [E, D]); din("Wg2", [D, E])
    din("gammaB", [128, E]); din("betaB", [128, E])
    din("rwT", [E, 8])
    din("rgT", [IR, E]); din("ruT", [IR, E]); din("rd", [IR, E])
    din("sgT", [IS, E]); din("suT", [IS, E]); din("sd", [IS, E])
    din("C_maskT", [CH, H * CH]); din("C_lamqF", [E, CH]); din("C_lamkT", [CH, E])
    din("C_pfx", [128, 128]); din("C_id", [128, 128]); din("C_iota8", [128, 8])
    din("C_bmi", [128, 8]); din("C_mye", [128, 1]); din("C_prefm", [8, 128])
    din("C_idsP", [128, 128]); din("C_onesr", [1, 128]); din("C_onesc", [128, 1])
    din("C_iota640", [128, 640])
    Y = nc.dram_tensor("Y", [T, E], F32, kind="ExternalOutput").ap()

    X1 = nc.dram_tensor("X1", [T, E], F32).ap()
    HS = [nc.dram_tensor(f"HS{s}", [IS, 512], BF16).ap() for s in range(4)]
    xf_in = nc.dram_tensor("xf_in", [T, E], BF16)
    xf_all = nc.dram_tensor("xf_all", [G, E], BF16)
    cnt_in = nc.dram_tensor("cnt_in", [1, 8], F32)
    cnt_all = nc.dram_tensor("cnt_all", [8, 8], F32)
    meta_in = nc.dram_tensor("meta_in", [T, 4], F32)
    meta_all = nc.dram_tensor("meta_all", [G, 4], F32)
    disp = nc.dram_tensor("disp", [2688, 1], F32)
    HE = nc.dram_tensor("HE", [IR, CAP], BF16)
    rout_in = nc.dram_tensor("rout_in", [CAP1, E], BF16)
    rout_all = nc.dram_tensor("rout_all", [NCORE * CAP1, E], BF16)

    lam_c = np.exp(-_slopes() * CH).astype(np.float32)

    with ExitStack() as ctx:
        EC = ctx.enter_context

        xn = EC(nc.sbuf_tensor("xn", [128, NT * E], BF16))
        maskT = EC(nc.sbuf_tensor("maskT", [128, H * CH], BF16))
        lamqF = EC(nc.sbuf_tensor("lamqF", [128, KC * CH], BF16))
        lamkT = EC(nc.sbuf_tensor("lamkT", [128, E], BF16))
        pfx = EC(nc.sbuf_tensor("pfx", [128, 128], F32))
        id32 = EC(nc.sbuf_tensor("id32", [128, 128], F32))
        idbf = EC(nc.sbuf_tensor("idbf", [128, 128], BF16))
        iota8 = EC(nc.sbuf_tensor("iota8", [128, 8], F32))
        iota640 = EC(nc.sbuf_tensor("iota640", [128, 640], F32))
        bmi = EC(nc.sbuf_tensor("bmi", [128, 8], F32))
        mye = EC(nc.sbuf_tensor("mye", [128, 1], F32))
        prefm = EC(nc.sbuf_tensor("prefm", [8, 128], F32))
        idsP = EC(nc.sbuf_tensor("idsP", [128, 128], F32))
        onesr = EC(nc.sbuf_tensor("onesr", [1, 128], F32))
        onesc = EC(nc.sbuf_tensor("onesc", [128, 1], F32))
        gammaB = EC(nc.sbuf_tensor("gammaB_s", [128, E], BF16))
        betaB = EC(nc.sbuf_tensor("betaB_s", [128, E], BF16))
        rwTs = EC(nc.sbuf_tensor("rwTs", [128, KC * 8], BF16))
        zer = EC(nc.sbuf_tensor("zer", [128, 64], F32))
        zbf = EC(nc.sbuf_tensor("zbf", [1, E], BF16))
        stat = EC(nc.sbuf_tensor("stat", [128, 16 * 8], F32))
        lg = EC(nc.sbuf_tensor("lg", [128, 128], F32))
        m8a = EC(nc.sbuf_tensor("m8a", [128, 128], F32))
        Ma = EC(nc.sbuf_tensor("Ma", [128, 128], F32))
        ranka = EC(nc.sbuf_tensor("ranka", [128, 128], F32))
        i1a = EC(nc.sbuf_tensor("i1a", [128, 16], F32))
        i2a = EC(nc.sbuf_tensor("i2a", [128, 16], F32))
        loca = EC(nc.sbuf_tensor("loca", [128, 32], I32))
        meta = EC(nc.sbuf_tensor("meta", [128, 64], F32))
        Rrow = EC(nc.sbuf_tensor("Rrow", [1, 8], F32))
        cnt8 = EC(nc.sbuf_tensor("cnt8", [8, 8], F32))
        Osb = EC(nc.sbuf_tensor("Osb", [128, 8], F32))
        dispi = EC(nc.sbuf_tensor("dispi", [128, 20], I32))
        t8a = EC(nc.sbuf_tensor("t8a", [128, 8], F32))
        t8b = EC(nc.sbuf_tensor("t8b", [128, 8], F32))
        t1a = EC(nc.sbuf_tensor("t1a", [128, 1], F32))
        t1b = EC(nc.sbuf_tensor("t1b", [128, 1], F32))
        epsln = EC(nc.sbuf_tensor("epsln", [128, 1], F32))
        epsrm = EC(nc.sbuf_tensor("epsrm", [128, 1], F32))
        sqs = EC(nc.sbuf_tensor("sqs", [128, E], F32))

        pf = EC(nc.psum_tensor("pf", [128, 3072], F32))
        pb = EC(nc.psum_tensor("pb", [128, 1024], BF16))
        pb2 = EC(nc.psum_tensor("pb2", [128, 1024], BF16))

        SEMS = []

        class Sem:
            def __init__(self, name):
                self.h = EC(nc.semaphore(name))
                self.v = 0
                SEMS.append(self)

            def inc(self, instr, k=1):
                instr.then_inc(self.h, k)
                self.v += k
                return self.v

            def tok(self):
                return (self.h, self.v)

        class Ring:
            def __init__(self, name, nslots):
                self.s = [Sem(f"{name}{i}") for i in range(nslots)]

            def dma(self, eng, slot, out, in_, **kw):
                sm = self.s[slot]
                sm.inc(eng.dma_start(out=out, in_=in_, **kw), 16)
                return sm.tok()

        def W(e, tok):
            e.wait_ge(tok[0], tok[1])

        sY = Sem("sY")
        sG = Sem("sG")
        sC = Sem("sC")
        sO = Sem("sO")
        sT = Sem("sT")
        sV = Sem("sV")
        sS = Sem("sS")
        sWg = Sem("sWg")
        sX0 = Sem("sX0")
        sX1 = Sem("sX1")
        sSC = Sem("sSC")
        sRD = Sem("sRD")
        rLn = Ring("rLn", 2)
        rWb = Ring("rWb", 2)
        rGU = Ring("rGU", 2)
        rWD = Ring("rWD", 4)
        rWD2 = Ring("rWD2", 2)
        rHE = Ring("rHE", 2)
        rHI = Ring("rHI", 4)
        rXG = Ring("rXG", 4)
        rRR = Ring("rRR", 2)
        rXI = Ring("rXI", 2)
        rER = Ring("rER", 1)
        rWT = Ring("rWT", 2)
        rHS = Ring("rHS", 2)
        sWC = Sem("sWC")
        sP = Sem("sP")
        wt_guard = [0]

        class PsumPool:
            def __init__(self, pf, nbank=6):
                self.pf = pf
                self.n = nbank
                self.busy = [False] * nbank
                self.free_at = [None] * nbank

            def acquire(self, eng, nb=1):
                for b0 in range(self.n - nb + 1):
                    if not any(self.busy[b0:b0 + nb]):
                        for b in range(b0, b0 + nb):
                            fa = self.free_at[b]
                            if fa is not None:
                                eng.wait_ge(fa[0], fa[1])
                                self.free_at[b] = None
                            self.busy[b] = True
                        return b0, self.pf[:, b0 * 512:(b0 + nb) * 512]
                raise RuntimeError("psum pool exhausted")

            def release(self, b0, nb, sem, val):
                for b in range(b0, b0 + nb):
                    self.busy[b] = False
                    self.free_at[b] = (sem, val)

        pool = PsumPool(pf)
        block = EC(nc.Block())

        _SYNC_OPS = frozenset((
            "tensor_scalar", "tensor_tensor", "tensor_add", "tensor_sub",
            "tensor_mul", "tensor_max", "tensor_copy", "tensor_scalar_add",
            "tensor_scalar_sub", "tensor_scalar_mul", "tensor_scalar_max",
            "tensor_scalar_min", "tensor_single_scalar", "scalar_tensor_tensor",
            "reduce_sum", "reduce_max", "tensor_reduce", "reciprocal", "max",
            "memset", "memzero", "select", "activation", "copy", "mul", "add",
            "square", "sqrt", "tensor_relu", "copy_predicated"))

        _LASTW = {"v": 0, "s": 0}

        class SyncedEng:
            def __init__(self, e, tag):
                self._e = e
                self._tag = tag

            def __getattr__(self, name):
                f = getattr(self._e, name)
                if name not in _SYNC_OPS:
                    return f

                def g(*a, **k):
                    if _LASTW[self._tag] < 256:
                        self._e.drain()
                    try:
                        w = int(k["out"].shape[-1])
                    except Exception:
                        w = 0
                    if "accum_out" in k and k["accum_out"] is not None:
                        try:
                            w = min(w, int(k["accum_out"].shape[-1]))
                        except Exception:
                            w = 0
                    _LASTW[self._tag] = w
                    return f(*a, **k)
                return g

        def BV(f):
            @block.vector
            def _(e):
                f(SyncedEng(e, "v"))

        def BS(f):
            @block.scalar
            def _(e):
                f(SyncedEng(e, "s"))

        def barrier():
            vals = [(s, s.v) for s in SEMS if s.v > 0]

            def emit(eng, own):
                for s, v in vals:
                    if s is not own:
                        eng.wait_ge(s.h, v)

            @block.sync
            def _(e):
                emit(e, sY)

            @block.gpsimd
            def _(e):
                emit(e, sG)

            @block.tensor
            def _(e):
                emit(e, sT)

            @BV
            def _(e):
                emit(e, sV)

            @BS
            def _(e):
                emit(e, sS)

        SUM, SQ, MU, RS, VAR, RSR, W1, W2 = range(8)

        def st(col, t0=0, n=16):
            return stat[:, col * 16 + t0: col * 16 + t0 + n]

        def fchunk(buf, width):
            return buf[:, :].rearrange("p (a c) -> p a c", c=width)

        _sqs_mark = [0]

        def sqs_square(e, in_ap, accum_ap):
            if _sqs_mark[0]:
                e.wait_ge(sS.h, _sqs_mark[0])
            r = sS.inc(e.activation(out=sqs[:, :], in_=in_ap, func=AF.Square,
                                    accum_out=accum_ap))
            _sqs_mark[0] = r
            return r

        def scope(name):
            sid, _ = nc.enter_named_scope(name, False)
            return (name, sid)

        def escope(s):
            nc.leave_named_scope(s[0], s[1], False)

        _sc = [scope("P01")]
        @block.sync
        def _(e):
            for n, d in (("C_pfx", pfx), ("C_id", id32), ("C_iota8", iota8),
                         ("C_bmi", bmi), ("C_mye", mye), ("C_prefm", prefm),
                         ("C_idsP", idsP), ("C_onesr", onesr), ("C_onesc", onesc),
                         ("C_iota640", iota640)):
                sY.inc(e.dma_start(out=d[:, :], in_=ap[n]), 16)
            e.wait_ge(sY.h, sY.v)

        @block.gpsimd
        def _(e):
            sG.inc(e.dma_start(out=maskT[:, :], in_=ap["C_maskT"]), 16)
            sG.inc(e.dma_start(out=lamqF[:, :],
                               in_=ap["C_lamqF"].rearrange("(a b) c -> b a c", b=128)), 16)
            sG.inc(e.dma_start(out=lamkT[:, :], in_=ap["C_lamkT"]), 16)
            sG.inc(e.dma_start(out=gammaB[:, :], in_=ap["gammaB"]), 16)
            sG.inc(e.dma_start(out=betaB[:, :], in_=ap["betaB"]), 16)
            sG.inc(e.dma_start(out=rwTs[:, :],
                               in_=ap["rwT"].rearrange("(a b) c -> b a c", b=128)), 16)
            sG.inc(e.dma_start(out=idbf[:, :], in_=ap["C_id"]), 16)
            e.wait_ge(sG.h, sG.v)

        @BV
        def _(e):
            e.memset(zer[:, :], 0.0)
            e.memset(epsln[:, :], EPS_LN)
            e.memset(epsrm[:, :], EPS_RMS)
            sV.inc(e.memset(zbf[:, :], 0.0))

        barrier()

        _ln_n = [0]

        def emit_ln(src_dram, out_sb, do_rms):
            _ln_n[0] += 1
            lnctx = ExitStack()
            lnin = lnctx.enter_context(
                nc.sbuf_tensor("lnin%d" % _ln_n[0], [128, 2 * E], F32))
            vm, sm = [], []
            toks = []
            for t in range(NT):
                @block.sync
                def _(e, t=t):
                    if t >= 2:
                        e.wait_ge(sV.h, vm[t - 2])
                        e.wait_ge(sS.h, sm[t - 2])
                    toks.append(rLn.dma(e, t % 2,
                                        lnin[:, (t % 2) * E:(t % 2) * E + E],
                                        src_dram.rearrange("(a b) c -> a b c", b=128)[t]))

                @BV
                def _(e, t=t):
                    W(e, toks[t])
                    vm.append(sV.inc(e.reduce_sum(out=st(SUM, t, 1),
                                                  in_=lnin[:, (t % 2) * E:(t % 2) * E + E],
                                                  axis=AXX)))

                @BS
                def _(e, t=t):
                    W(e, toks[t])
                    sm.append(sqs_square(e, lnin[:, (t % 2) * E:(t % 2) * E + E],
                                         st(SQ, t, 1)))

            @BV
            def _(e):
                e.wait_ge(sS.h, sm[-1])
                e.tensor_scalar(out=st(MU), in0=st(SUM), scalar1=1.0 / E, scalar2=None,
                                op0=ALU.mult)
                e.tensor_scalar(out=st(VAR), in0=st(SQ), scalar1=1.0 / E, scalar2=None,
                                op0=ALU.mult)
                e.tensor_tensor(out=st(RS), in0=st(MU), in1=st(MU), op=ALU.mult)
                sV.inc(e.tensor_tensor(out=st(VAR), in0=st(VAR), in1=st(RS), op=ALU.subtract))

            @BS
            def _(e):
                e.wait_ge(sV.h, sV.v)
                sS.inc(e.activation(out=st(RS), in_=st(VAR), func=AF.Sqrt, bias=epsln[:, 0:1]))

            @BV
            def _(e):
                e.wait_ge(sS.h, sS.v)
                if do_rms:
                    e.reciprocal(out=st(RS), in_=st(RS))
                    e.tensor_tensor(out=st(RSR), in0=st(VAR), in1=st(RS), op=ALU.mult)
                    sV.inc(e.tensor_tensor(out=st(RSR), in0=st(RSR), in1=st(RS), op=ALU.mult))
                else:
                    sV.inc(e.reciprocal(out=st(RS), in_=st(RS)))

            if do_rms:
                @BS
                def _(e):
                    e.wait_ge(sV.h, sV.v)
                    sS.inc(e.activation(out=st(RSR), in_=st(RSR), func=AF.Sqrt,
                                        bias=epsrm[:, 0:1]))

                @BV
                def _(e):
                    e.wait_ge(sS.h, sS.v)
                    sV.inc(e.reciprocal(out=st(RSR), in_=st(RSR)))

            am = []
            toks2 = []
            for t in range(NT):
                @block.sync
                def _(e, t=t):
                    if t < 2:
                        e.wait_ge(sV.h, vm[-1])
                        e.wait_ge(sS.h, sm[-1])
                    else:
                        e.wait_ge(sV.h, am[t - 2])
                    toks2.append(rLn.dma(e, t % 2,
                                         lnin[:, (t % 2) * E:(t % 2) * E + E],
                                         src_dram.rearrange("(a b) c -> a b c", b=128)[t]))

                @BV
                def _(e, t=t):
                    W(e, toks2[t])
                    am.append(sV.inc(e.tensor_scalar(
                        out=out_sb[:, t * E:(t + 1) * E],
                        in0=lnin[:, (t % 2) * E:(t % 2) * E + E],
                        scalar1=st(MU, t, 1), scalar2=st(RS, t, 1),
                        op0=ALU.subtract, op1=ALU.mult)))
            lnctx.close()

        def emit_t2f(src_ap_fn, evac_out_ap):
            @block.tensor
            def _(e):
                e.wait_ge(sV.h, sV.v)
                for fc in range(KC):
                    mm = e.transpose(out=pb[:, fc * 128:(fc + 1) * 128],
                                     in_=src_ap_fn(fc), identity=idbf[:, :])
                sT.inc(mm)

            @BV
            def _(e):
                e.wait_ge(sT.h, sT.v)
                sV.inc(e.tensor_copy(out=evac_out_ap,
                                     in_=pb[:, :].rearrange("p (a c) -> p a c", a=KC)))

        emit_ln(ap["X"], xn, do_rms=(upto == "st1"))
        barrier()
        if upto == "st1":
            @BS
            def _(e):
                sO.inc(e.dma_start(out=Y[0:128, 0:128], in_=stat[:, :]), 16)
                e.wait_ge(sO.h, sO.v)
            return nc

        def dump_bf16_to_Y(src_sb):
            dbg = EC(nc.sbuf_tensor("dbgf32", [128, 2 * E], F32))
            dmk = []
            for t in range(NT):
                @BV
                def _(e, t=t):
                    if t >= 2:
                        e.wait_ge(sO.h, dmk[t - 2])
                    sV.inc(e.tensor_copy(out=dbg[:, (t % 2) * E:(t % 2) * E + E],
                                         in_=src_sb[:, t * E:(t + 1) * E]))

                @BS
                def _(e, t=t):
                    e.wait_ge(sV.h, sV.v)
                    sO.inc(e.dma_start(out=Y.rearrange("(a b) c -> a b c", b=128)[t],
                                       in_=dbg[:, (t % 2) * E:(t % 2) * E + E]), 16)
                    e.wait_ge(sO.h, sO.v)
                    dmk.append(sO.v)

        def dump_dram_to_Y(src_dram):
            dbg = EC(nc.sbuf_tensor("dbgf32b", [128, 2 * E], F32))
            dmk = []
            for t in range(NT):
                tok = [None]

                @block.sync
                def _(e, t=t):
                    if t >= 2:
                        e.wait_ge(sO.h, dmk[t - 2])
                    tok[0] = rXI.dma(e, t % 2, dbg[:, (t % 2) * E:(t % 2) * E + E],
                                     src_dram.rearrange("(a b) c -> a b c", b=128)[t])

                @BS
                def _(e, t=t):
                    W(e, tok[0])
                    sO.inc(e.dma_start(out=Y.rearrange("(a b) c -> a b c", b=128)[t],
                                       in_=dbg[:, (t % 2) * E:(t % 2) * E + E]), 16)
                    e.wait_ge(sO.h, sO.v)
                    dmk.append(sO.v)

        if upto == "ln1":
            dump_bf16_to_Y(xn)
            return nc

        def dbg_end():
            @BS
            def _(e):
                sO.inc(e.dma_start(out=Y[0:128, 0:128], in_=stat[:, :]), 16)
                e.wait_ge(sO.h, sO.v)


        escope(_sc[0])
        _sc[0] = scope("P2")
        with ExitStack() as p2:
            E2 = p2.enter_context
            wbuf = E2(nc.sbuf_tensor("wbuf", [128, 2 * E], BF16))
            wg1s = E2(nc.sbuf_tensor("wg1s", [128, KC * D], BF16))
            wg2s = E2(nc.sbuf_tensor("wg2s", [64, E], BF16))
            xnF = E2(nc.sbuf_tensor("xnF", [128, KC * SLAB], BF16))
            qF = E2(nc.sbuf_tensor("qF", [128, KC * SLAB], BF16))
            ktok = E2(nc.sbuf_tensor("ktok", [128, 4 * E], BF16))
            vtok = E2(nc.sbuf_tensor("vtok", [128, 4 * E], BF16))
            ogF = E2(nc.sbuf_tensor("ogF", [128, KC * SLAB], BF16))
            kFb = E2(nc.sbuf_tensor("kFb", [128, KC * CH], BF16))
            qdec = E2(nc.sbuf_tensor("qdec", [128, KC * CH], BF16))
            kdec = E2(nc.sbuf_tensor("kdec", [128, E], BF16))
            msb = E2(nc.sbuf_tensor("msb", [128, 2 * CH], BF16))
            g1F = E2(nc.sbuf_tensor("g1F", [64, SLAB], BF16))
            gate = E2(nc.sbuf_tensor("gate", [128, E], BF16))
            tA = E2(nc.sbuf_tensor("tA", [128, E], BF16))
            tB = E2(nc.sbuf_tensor("tB", [128, E], BF16))
            kvs = E2(nc.sbuf_tensor("kvs", [128, (H // 2) * D], F32))
            kvb = E2(nc.sbuf_tensor("kvb", [128, (H // 2) * D], BF16))
            x0t = E2(nc.sbuf_tensor("x0t", [128, 4 * E], F32))
            x1t = E2(nc.sbuf_tensor("x1t", [128, 4 * E], F32))

            @BV
            def _(e):
                e.memset(kvs[:, :], 0.0)
                sV.inc(e.memset(kvb[:, :], 0.0))

            @block.gpsimd
            def _(e):
                sWg.inc(e.dma_start(out=wg1s[:, :],
                                    in_=ap["Wg1"].rearrange("(a b) c -> b a c", b=128)), 16)
                sWg.inc(e.dma_start(out=wg2s[:, :], in_=ap["Wg2"]), 16)

            wb_guard = [0]

            def wsweep(wname, body):
                kmarks = []
                for k in range(KC):
                    tk = [None]

                    @block.gpsimd
                    def _(e, k=k):
                        if k >= 2:
                            e.wait_ge(sT.h, kmarks[k - 2])
                        elif wb_guard[0]:
                            e.wait_ge(sT.h, wb_guard[0])
                        tk[0] = rWb.dma(e, k % 2,
                                        wbuf[:, (k % 2) * E:(k % 2) * E + E],
                                        ap[wname][k * 128:(k + 1) * 128, :])

                    @block.tensor
                    def _(e, k=k):
                        W(e, tk[0])
                        body(e, k, wbuf[:, (k % 2) * E:(k % 2) * E + E])
                        kmarks.append(sT.v)
                wb_guard[0] = kmarks[-1]

            for sl in range(4):
                c0 = sl * 4
                tok0 = sl * SLAB

                for cc in range(4):
                    tl = c0 + cc
                    emit_t2f(lambda fc, tl=tl: xn[:, tl * E + fc * 128: tl * E + (fc + 1) * 128],
                             xnF[:, :].rearrange("p (a b c) -> p a b c", a=KC, b=4)[:, :, cc, :])

                qbanks = {}

                def qbody_factory(mg):
                    def qbody(e, k, wk):
                        if k == 0:
                            e.wait_ge(sV.h, sV.v)
                            for mi in range(4):
                                qbanks[(mg, mi)] = pool.acquire(e, 1)
                        for mi in range(4):
                            m = mg * 4 + mi
                            b0, pp = qbanks[(mg, mi)]
                            sT.inc(e.matmul(out=pp[:, 0:SLAB],
                                            lhsT=wk[:, m * 128:(m + 1) * 128],
                                            rhs=fchunk(xnF, SLAB)[:, k, :],
                                            start=(k == 0), stop=(k == KC - 1)))
                    return qbody

                for mg in range(2):
                    wsweep("Wq", qbody_factory(mg))

                    @BS
                    def _(e, mg=mg):
                        e.wait_ge(sT.h, sT.v)
                        for mi in range(4):
                            m = mg * 4 + mi
                            b0, pp = qbanks[(mg, mi)]
                            v = sS.inc(e.activation(out=qF[:, m * SLAB:(m + 1) * SLAB],
                                                    in_=pp[:, 0:SLAB], func=AF.Silu))
                            pool.release(b0, 1, sS.h, v)

                for wname, dst in (("Wk", ktok), ("Wv", vtok)):
                    for wave in range(2):
                        kvbanks = {}

                        def kvbody(e, k, wk, wave=wave):
                            if k == 0:
                                for tci in range(2):
                                    kvbanks[tci] = pool.acquire(e, 2)
                            for tci in range(2):
                                tc = wave * 2 + tci
                                b0, pp = kvbanks[tci]
                                for half in range(2):
                                    sT.inc(e.matmul(
                                        out=pp[:, half * 512:(half + 1) * 512],
                                        lhsT=fchunk(xnF, SLAB)[:, k, tc * 128:(tc + 1) * 128],
                                        rhs=wk[:, half * 512:(half + 1) * 512],
                                        start=(k == 0), stop=(k == KC - 1)))

                        wsweep(wname, kvbody)

                        @BS
                        def _(e, wave=wave, dst=dst):
                            e.wait_ge(sT.h, sT.v)
                            for tci in range(2):
                                tc = wave * 2 + tci
                                b0, pp = kvbanks[tci]
                                v = sS.inc(e.activation(out=dst[:, tc * E:(tc + 1) * E],
                                                        in_=pp[:, 0:E], func=AF.Silu))
                                pool.release(b0, 2, sS.h, v)

                g1bank = []

                @block.tensor
                def _(e):
                    b0, pp = pool.acquire(e, 1)
                    g1bank.append((b0, pp))
                    e.wait_ge(sWg.h, 32)
                    for k in range(KC):
                        sT.inc(e.matmul(out=pp[0:64, 0:SLAB],
                                        lhsT=wg1s[:, k * D:(k + 1) * D],
                                        rhs=fchunk(xnF, SLAB)[:, k, :],
                                        start=(k == 0), stop=(k == KC - 1)))

                @BS
                def _(e):
                    b0, pp = g1bank[0]
                    e.wait_ge(sT.h, sT.v)
                    v = sS.inc(e.activation(out=g1F[:, :], in_=pp[0:64, 0:SLAB],
                                            func=AF.Identity))
                    pool.release(b0, 1, sS.h, v)

                if upto == "p2s0q":
                    break

                @block.sync
                def _(e, c0=c0, sl=sl):
                    if sl:
                        e.wait_ge(sO.h, sO.v)
                    for cc in range(4):
                        sX0.inc(e.dma_start(out=x0t[:, cc * E:(cc + 1) * E],
                                            in_=ap["X"].rearrange("(a b) c -> a b c", b=128)[c0 + cc]), 16)
                x0tok = sX0.tok()

                for cc in range(4):
                    emit_t2f(lambda fc, cc=cc: ktok[:, cc * E + fc * 128: cc * E + (fc + 1) * 128],
                             kFb[:, :].rearrange("p (a c) -> p a c", a=KC))

                    @BV
                    def _(e, cc=cc):
                        sV.inc(e.tensor_tensor(out=kdec[:, :], in0=ktok[:, cc * E:(cc + 1) * E],
                                               in1=lamkT[:, :], op=ALU.mult))
                        for fc in range(KC):
                            sV.inc(e.tensor_tensor(
                                out=qdec[:, fc * CH:(fc + 1) * CH],
                                in0=qF[:, fc * SLAB + cc * CH: fc * SLAB + (cc + 1) * CH],
                                in1=lamqF[:, fc * CH:(fc + 1) * CH], op=ALU.mult))

                    gbank = []

                    @block.tensor
                    def _(e, cc=cc):
                        b0, pp = pool.acquire(e, 2)
                        gbank.append((b0, pp))
                        e.wait_ge(sS.h, sS.v)
                        for half in range(2):
                            sT.inc(e.matmul(out=pp[:, half * 512:(half + 1) * 512],
                                            lhsT=g1F[:, cc * CH:(cc + 1) * CH],
                                            rhs=wg2s[:, half * 512:(half + 1) * 512],
                                            start=True, stop=True))

                    @BS
                    def _(e):
                        b0, pp = gbank[0]
                        e.wait_ge(sT.h, sT.v)
                        v = sS.inc(e.activation(out=gate[:, :], in_=pp[:, 0:E],
                                                func=AF.Sigmoid))
                        pool.release(b0, 2, sS.h, v)

                    obank = []

                    @block.tensor
                    def _(e):
                        obank.append(pool.acquire(e, 2))
                        obank.append(pool.acquire(e, 2))

                    mmarks = []
                    kvm = []
                    for h in range(H):
                        fc, ro = h // 2, (h % 2) * D

                        @block.tensor
                        def _(e, h=h, fc=fc, ro=ro, cc=cc):
                            b1, p1 = obank[1]
                            if h == 0:
                                e.wait_ge(sV.h, sV.v)
                            else:
                                e.wait_ge(sV.h, mmarks[h - 1])
                            sT.inc(e.matmul(out=p1[:, 0:CH],
                                            lhsT=kFb[ro:ro + D, fc * CH:(fc + 1) * CH],
                                            rhs=qF[ro:ro + D, fc * SLAB + cc * CH: fc * SLAB + (cc + 1) * CH],
                                            start=True, stop=True))

                        scm = sT.v

                        @BV
                        def _(e, h=h, scm=scm):
                            b1, p1 = obank[1]
                            e.wait_ge(sT.h, scm)
                            mmarks.append(sV.inc(e.scalar_tensor_tensor(
                                out=msb[:, (h % 2) * CH:(h % 2) * CH + CH],
                                in0=p1[:, 0:CH], scalar=1.0,
                                in1=maskT[:, h * CH:(h + 1) * CH],
                                op0=ALU.mult, op1=ALU.mult)))

                        @block.tensor
                        def _(e, h=h, fc=fc, ro=ro, cc=cc):
                            b0, pp = obank[0]
                            b1, p1 = obank[1]
                            e.wait_ge(sV.h, mmarks[h])
                            if h > 0:
                                e.wait_ge(sV.h, kvm[h - 1])
                            sT.inc(e.matmul(out=pp[:, h * D:(h + 1) * D],
                                            lhsT=msb[:, (h % 2) * CH:(h % 2) * CH + CH],
                                            rhs=vtok[:, cc * E + h * D: cc * E + (h + 1) * D],
                                            start=(h == 0 or h == 8), stop=False))
                            sT.inc(e.matmul(out=pp[:, h * D:(h + 1) * D],
                                            lhsT=qdec[ro:ro + D, fc * CH:(fc + 1) * CH],
                                            rhs=kvb[ro:ro + D, (h // 2) * D:(h // 2 + 1) * D],
                                            start=False, stop=(h == 7 or h == 15)))
                            sT.inc(e.matmul(out=p1[ro:ro + D, 512:512 + D],
                                            lhsT=kdec[:, h * D:(h + 1) * D],
                                            rhs=vtok[:, cc * E + h * D: cc * E + (h + 1) * D],
                                            start=True, stop=True,
                                            tile_position=(0, ro)))

                        kvt = sT.v

                        @BV
                        def _(e, h=h, kvt=kvt):
                            b1, p1 = obank[1]
                            e.wait_ge(sT.h, kvt)
                            ro2 = (h % 2) * D
                            e.scalar_tensor_tensor(
                                out=kvs[ro2:ro2 + D, (h // 2) * D:(h // 2 + 1) * D],
                                in0=kvs[ro2:ro2 + D, (h // 2) * D:(h // 2 + 1) * D],
                                scalar=float(lam_c[h]),
                                in1=p1[ro2:ro2 + D, 512:512 + D],
                                op0=ALU.mult, op1=ALU.add)
                            kvm.append(sV.inc(e.tensor_copy(
                                out=kvb[ro2:ro2 + D, (h // 2) * D:(h // 2 + 1) * D],
                                in_=kvs[ro2:ro2 + D, (h // 2) * D:(h // 2 + 1) * D])))

                    if upto == "p2s0c0h" and cc == 0:
                        break

                    redm = [0]
                    if upto != "p2e0b":
                        @BV
                        def _(e):
                            b0, pp = obank[0]
                            e.wait_ge(sT.h, sT.v)
                            redm[0] = sV.inc(e.reduce_sum(out=t1a[:, :],
                                                          in_=pp[:, 0:E], axis=AXX))

                    if upto == "p2e0a" and cc == 0:
                        break

                    @BS
                    def _(e):
                        b0, pp = obank[0]
                        e.wait_ge(sT.h, sT.v)
                        if redm[0]:
                            e.wait_ge(sV.h, redm[0])
                        sqs_square(e, pp[:, 0:E], t1b[:, :])

                    if upto == "p2e0b" and cc == 0:
                        break

                    if upto == "p2e1" and cc == 0:
                        break

                    @BV
                    def _(e):
                        e.wait_ge(sS.h, sS.v)
                        e.tensor_scalar(out=t1a[:, :], in0=t1a[:, :], scalar1=1.0 / E,
                                        scalar2=None, op0=ALU.mult)
                        e.tensor_scalar(out=t1b[:, :], in0=t1b[:, :], scalar1=1.0 / E,
                                        scalar2=None, op0=ALU.mult)
                        e.tensor_scalar(out=t8a[:, 0:1], in0=t1a[:, :], scalar1=t1a[:, 0:1],
                                        scalar2=None, op0=ALU.mult)
                        sV.inc(e.tensor_tensor(out=t1b[:, :], in0=t1b[:, :],
                                               in1=t8a[:, 0:1], op=ALU.subtract))

                    if upto == "p2e2" and cc == 0:
                        break

                    @BS
                    def _(e):
                        e.wait_ge(sV.h, sV.v)
                        sS.inc(e.activation(out=t1b[:, :], in_=t1b[:, :], func=AF.Sqrt,
                                            bias=epsln[:, 0:1]))

                    if upto == "p2e3" and cc == 0:
                        break

                    @BV
                    def _(e):
                        b0, pp = obank[0]
                        e.wait_ge(sS.h, sS.v)
                        e.reciprocal(out=t1b[:, :], in_=t1b[:, :])
                        e.tensor_scalar(out=tA[:, :], in0=pp[:, 0:E],
                                        scalar1=t1a[:, 0:1], scalar2=t1b[:, 0:1],
                                        op0=ALU.subtract, op1=ALU.mult)
                        e.tensor_tensor(out=tB[:, :], in0=tA[:, :], in1=gammaB[:, :], op=ALU.mult)
                        e.tensor_tensor(out=tA[:, :], in0=tB[:, :], in1=betaB[:, :], op=ALU.add)
                        v = sV.inc(e.tensor_tensor(out=tB[:, :], in0=tA[:, :],
                                                   in1=gate[:, :], op=ALU.mult))
                        pool.release(obank[0][0], 2, sV.h, v)
                        pool.release(obank[1][0], 2, sV.h, v)

                    if upto == "p2e4" and cc == 0:
                        break

                    emit_t2f(lambda fc: tB[:, fc * 128:(fc + 1) * 128],
                             ogF[:, :].rearrange("p (a b c) -> p a b c", a=KC, b=4)[:, :, cc, :])

                    if upto == "p2s0c0e" and cc == 0:
                        break

                if upto in ("p2s0c0h", "p2s0c0e", "p2e0a", "p2e0b",
                            "p2e1", "p2e2", "p2e3", "p2e4"):
                    break

                for wave in range(2):
                    xbanks = {}

                    def obody(e, k, wk, wave=wave):
                        if k == 0:
                            e.wait_ge(sV.h, sV.v)
                            for tci in range(2):
                                xbanks[tci] = pool.acquire(e, 2)
                        for tci in range(2):
                            tc = wave * 2 + tci
                            b0, pp = xbanks[tci]
                            for half in range(2):
                                sT.inc(e.matmul(
                                    out=pp[:, half * 512:(half + 1) * 512],
                                    lhsT=fchunk(ogF, SLAB)[:, k, tc * 128:(tc + 1) * 128],
                                    rhs=wk[:, half * 512:(half + 1) * 512],
                                    start=(k == 0), stop=(k == KC - 1)))

                    wsweep("Wo", obody)

                    @BV
                    def _(e, wave=wave, x0tok=x0tok):
                        e.wait_ge(sT.h, sT.v)
                        W(e, x0tok)
                        for tci in range(2):
                            tc = wave * 2 + tci
                            b0, pp = xbanks[tci]
                            v = sV.inc(e.tensor_tensor(out=x1t[:, tc * E:(tc + 1) * E],
                                                       in0=x0t[:, tc * E:(tc + 1) * E],
                                                       in1=pp[:, 0:E], op=ALU.add))
                            pool.release(b0, 2, sV.h, v)

                    @BS
                    def _(e, wave=wave, c0=c0):
                        e.wait_ge(sV.h, sV.v)
                        for tci in range(2):
                            tc = wave * 2 + tci
                            sO.inc(e.dma_start(
                                out=X1.rearrange("(a b) c -> a b c", b=128)[c0 + tc],
                                in_=x1t[:, tc * E:(tc + 1) * E]), 16)
                        e.wait_ge(sO.h, sO.v)

                if upto == "p2s0":
                    break

        barrier()
        if upto in ("p2s0q", "p2s0", "p2s0c0h", "p2s0c0e", "p2e0a", "p2e0b",
                    "p2e1", "p2e2", "p2e3", "p2e4"):
            dbg_end()
            return nc
        if upto == "p2n":
            dbg_end()
            return nc
        if upto == "p2":
            dump_dram_to_Y(X1)
            return nc

        escope(_sc[0])
        _sc[0] = scope("P3a")
        emit_ln(X1, xn, do_rms=True)

        @block.gpsimd
        def _(e):
            e.wait_ge(sV.h, sV.v)
            sG.inc(e.dma_start(out=xf_in.ap().rearrange("(a b) c -> b a c", b=128),
                               in_=xn[:, :].rearrange("p (a c) -> p a c", a=NT)), 16)
            e.wait_ge(sG.h, sG.v)
            sC.inc(e.collective_compute(
                "AllGather", ALU.bypass, replica_groups=[list(range(NCORE))],
                ins=[xf_in.ap().opt()], outs=[xf_all.ap().opt()]))

        xfF = EC(nc.sbuf_tensor("xfF", [128, KC * T], BF16))
        for tl in range(NT):
            emit_t2f(lambda fc, tl=tl: xn[:, tl * E + fc * 128: tl * E + (fc + 1) * 128],
                     xfF[:, :].rearrange("p (a b c) -> p a b c", a=KC, b=NT)[:, :, tl, :])

        escope(_sc[0])
        _sc[0] = scope("P3b")
        lgF = EC(nc.sbuf_tensor("lgF", [8, T], F32))
        rb = []

        @block.tensor
        def _(e):
            b0, pp = pool.acquire(e, 4)
            rb.append((b0, pp))
            e.wait_ge(sV.h, sV.v)
            for k in range(KC):
                for q in range(4):
                    sT.inc(e.matmul(out=pp[0:8, q * 512:(q + 1) * 512],
                                    lhsT=rwTs[:, k * 8:(k + 1) * 8],
                                    rhs=fchunk(xfF, T)[:, k, q * 512:(q + 1) * 512],
                                    start=(k == 0), stop=(k == KC - 1)))

        @BV
        def _(e):
            b0, pp = rb[0]
            e.wait_ge(sT.h, sT.v)
            v = sV.inc(e.tensor_copy(out=lgF[:, :], in_=pp[0:8, 0:T]))
            pool.release(b0, 4, sV.h, v)

        rb2 = []

        @block.tensor
        def _(e):
            b0, pp = pool.acquire(e, 1)
            rb2.append((b0, pp))
            e.wait_ge(sV.h, sV.v)
            for tl in range(NT):
                mm = e.transpose(out=pp[:, tl * 8:(tl + 1) * 8],
                                 in_=lgF[:, tl * 128:(tl + 1) * 128],
                                 identity=id32[0:8, 0:8])
            sT.inc(mm)

        @BV
        def _(e):
            b0, pp = rb2[0]
            e.wait_ge(sT.h, sT.v)
            v = None
            for tl in range(NT):
                v = sV.inc(e.tensor_scalar(out=lg[:, tl * 8:(tl + 1) * 8],
                                           in0=pp[:, tl * 8:(tl + 1) * 8],
                                           scalar1=st(RSR, tl, 1), scalar2=None,
                                           op0=ALU.mult))
            pool.release(b0, 1, sV.h, v)

        @BV
        def _(e):
            for tl in range(NT):
                lgt = lg[:, tl * 8:(tl + 1) * 8]
                m1 = m8a[:, tl * 8: tl * 8 + 1]
                m2 = m8a[:, tl * 8 + 1: tl * 8 + 2]
                e.max(out=m8a[:, tl * 8:(tl + 1) * 8], in_=lgt)
                e.tensor_scalar(out=t8a[:, :], in0=lgt, scalar1=m1, scalar2=None,
                                op0=ALU.is_equal)
                e.tensor_tensor(out=t8a[:, :], in0=t8a[:, :], in1=bmi[:, :], op=ALU.mult)
                e.reduce_max(out=t1a[:, :], in_=t8a[:, :], axis=AXX)
                e.tensor_scalar(out=i1a[:, tl:tl + 1], in0=t1a[:, :], scalar1=-1.0,
                                scalar2=BIG, op0=ALU.mult, op1=ALU.add)
                e.tensor_scalar(out=t8a[:, :], in0=iota8[:, :], scalar1=i1a[:, tl:tl + 1],
                                scalar2=None, op0=ALU.is_equal)
                e.tensor_copy(out=Ma[:, tl * 8:(tl + 1) * 8], in_=t8a[:, :])
                e.tensor_scalar(out=t8b[:, :], in0=lgt, scalar1=m2, scalar2=None,
                                op0=ALU.is_equal)
                e.tensor_tensor(out=t8b[:, :], in0=t8b[:, :], in1=bmi[:, :], op=ALU.mult)
                e.tensor_scalar(out=t8a[:, :], in0=t8a[:, :], scalar1=-BIG, scalar2=None,
                                op0=ALU.mult)
                e.tensor_tensor(out=t8b[:, :], in0=t8b[:, :], in1=t8a[:, :], op=ALU.add)
                e.reduce_max(out=t1a[:, :], in_=t8b[:, :], axis=AXX)
                e.tensor_scalar(out=i2a[:, tl:tl + 1], in0=t1a[:, :], scalar1=-1.0,
                                scalar2=BIG, op0=ALU.mult, op1=ALU.add)
                e.tensor_scalar(out=t8b[:, :], in0=iota8[:, :], scalar1=i2a[:, tl:tl + 1],
                                scalar2=None, op0=ALU.is_equal)
                e.tensor_tensor(out=Ma[:, tl * 8:(tl + 1) * 8],
                                in0=Ma[:, tl * 8:(tl + 1) * 8], in1=t8b[:, :], op=ALU.add)
                e.tensor_tensor(out=st(W1, tl, 1), in0=m1, in1=m2, op=ALU.subtract)
            sV.inc(e.tensor_copy(out=t1b[:, :], in_=t1a[:, :]))

        @BS
        def _(e):
            e.wait_ge(sV.h, sV.v)
            sS.inc(e.activation(out=st(W1), in_=st(W1), func=AF.Sigmoid))

        @BV
        def _(e):
            e.wait_ge(sS.h, sS.v)
            sV.inc(e.tensor_scalar(out=st(W2), in0=st(W1), scalar1=-1.0, scalar2=1.0,
                                   op0=ALU.mult, op1=ALU.add))

        rankb = []

        @block.tensor
        def _(e):
            rankb.append(pool.acquire(e, 1))
            rankb.append(pool.acquire(e, 1))

        @BV
        def _(e):
            sV.inc(e.memset(Rrow[:, :], 0.0))

        for tl in range(NT):
            @block.tensor
            def _(e, tl=tl):
                b0, pp = rankb[0]
                b1, p1 = rankb[1]
                e.wait_ge(sV.h, sV.v)
                sT.inc(e.matmul(out=pp[:, 0:8], lhsT=pfx[:, :],
                                rhs=Ma[:, tl * 8:(tl + 1) * 8], start=True, stop=False))
                sT.inc(e.matmul(out=pp[:, 0:8], lhsT=onesr[:, :], rhs=Rrow[:, :],
                                start=False, stop=True))
                sT.inc(e.matmul(out=p1[0:1, 0:8], lhsT=onesc[:, :],
                                rhs=Ma[:, tl * 8:(tl + 1) * 8],
                                start=True, stop=True))

            @BV
            def _(e, tl=tl):
                b0, pp = rankb[0]
                b1, p1 = rankb[1]
                e.wait_ge(sT.h, sT.v)
                e.tensor_copy(out=ranka[:, tl * 8:(tl + 1) * 8], in_=pp[:, 0:8])
                sV.inc(e.tensor_tensor(out=Rrow[:, :], in0=Rrow[:, :],
                                       in1=p1[0:1, 0:8], op=ALU.add))

        @block.gpsimd
        def _(e):
            e.wait_ge(sV.h, sV.v)
            sG.inc(e.dma_start(out=cnt_in[:, :], in_=Rrow[:, :]), 16)
            e.wait_ge(sG.h, sG.v)
            sC.inc(e.collective_compute(
                "AllGather", ALU.bypass, replica_groups=[list(range(NCORE))],
                ins=[cnt_in.ap().opt()], outs=[cnt_all.ap().opt()]))
            e.wait_ge(sC.h, sC.v)
            sG.inc(e.dma_start(out=cnt8[:, :], in_=cnt_all[:, :]), 16)
            e.wait_ge(sG.h, sG.v)

        @block.tensor
        def _(e):
            b0, pp = rankb[0]
            e.wait_ge(sG.h, sG.v)
            e.wait_ge(sV.h, sV.v)
            sT.inc(e.matmul(out=pp[:, 0:8], lhsT=prefm[:, :], rhs=cnt8[:, :],
                            start=True, stop=True))

        @BV
        def _(e):
            b0, pp = rankb[0]
            e.wait_ge(sT.h, sT.v)
            v = sV.inc(e.tensor_copy(out=Osb[:, :], in_=pp[:, 0:8]))
            pool.release(rankb[0][0], 1, sV.h, v)
            pool.release(rankb[1][0], 1, sV.h, v)

        @BV
        def _(e):
            for tl in range(NT):
                for which, ia in ((0, i1a), (1, i2a)):
                    e.tensor_scalar(out=t8a[:, :], in0=iota8[:, :],
                                    scalar1=ia[:, tl:tl + 1], scalar2=None, op0=ALU.is_equal)
                    e.tensor_tensor(out=t8b[:, :], in0=ranka[:, tl * 8:(tl + 1) * 8],
                                    in1=t8a[:, :], op=ALU.mult)
                    e.reduce_sum(out=t1a[:, :], in_=t8b[:, :], axis=AXX)
                    e.tensor_tensor(out=t8b[:, :], in0=Osb[:, :], in1=t8a[:, :], op=ALU.mult)
                    e.reduce_sum(out=t1b[:, :], in_=t8b[:, :], axis=AXX)
                    e.tensor_tensor(out=t1a[:, :], in0=t1a[:, :], in1=t1b[:, :], op=ALU.add)
                    e.tensor_copy(out=meta[:, tl * 4 + 2 * which: tl * 4 + 2 * which + 1],
                                  in_=ia[:, tl:tl + 1])
                    e.tensor_copy(out=meta[:, tl * 4 + 2 * which + 1: tl * 4 + 2 * which + 2],
                                  in_=t1a[:, :])
                    e.tensor_scalar(out=t1a[:, :], in0=t1a[:, :], scalar1=float(CAP),
                                    scalar2=None, op0=ALU.min)
                    e.tensor_scalar(out=t1b[:, :], in0=ia[:, tl:tl + 1],
                                    scalar1=float(CAP1), scalar2=None, op0=ALU.mult)
                    e.tensor_tensor(out=t1a[:, :], in0=t1a[:, :], in1=t1b[:, :], op=ALU.add)
                    e.tensor_copy(out=loca[:, which * 16 + tl: which * 16 + tl + 1],
                                  in_=t1a[:, :])
            sV.inc(e.tensor_copy(out=t1b[:, :], in_=t1a[:, :]))

        @block.gpsimd
        def _(e):
            e.wait_ge(sV.h, sV.v)
            sG.inc(e.dma_start(out=meta_in.ap().rearrange("(a b) c -> b a c", b=128),
                               in_=meta[:, :].rearrange("p (a c) -> p a c", a=NT)), 16)
            e.wait_ge(sG.h, sG.v)
            sC.inc(e.collective_compute(
                "AllGather", ALU.bypass, replica_groups=[list(range(NCORE))],
                ins=[meta_in.ap().opt()], outs=[meta_all.ap().opt()]))

        escope(_sc[0])
        _sc[0] = scope("P3c")
        mslab = EC(nc.sbuf_tensor("mslab", [128, 512], F32))
        slotv = EC(nc.sbuf_tensor("slotv", [128, 128], F32))
        sloti = EC(nc.sbuf_tensor("sloti", [128, 128], I32))
        tmA = EC(nc.sbuf_tensor("tmA", [128, 128], F32))
        tmB = EC(nc.sbuf_tensor("tmB", [128, 128], F32))
        tmC = EC(nc.sbuf_tensor("tmC", [128, 128], F32))

        @block.gpsimd
        def _(e):
            e.wait_ge(sC.h, sC.v)
            sG.inc(e.dma_start(out=mslab[:, :],
                               in_=meta_all.ap().rearrange("(a b) c -> a (b c)", a=128)), 16)
            e.wait_ge(sG.h, sG.v)

        @BV
        def _(e):
            e.wait_ge(sG.h, sG.v)
            mv = mslab[:, :].rearrange("p (c f) -> p f c", f=4)
            i1v, g1v, i2v, g2v = mv[:, 0, :], mv[:, 1, :], mv[:, 2, :], mv[:, 3, :]
            e.tensor_scalar(out=tmA[:, :], in0=i1v, scalar1=mye[:, 0:1], scalar2=None,
                            op0=ALU.is_equal)
            e.tensor_scalar(out=tmB[:, :], in0=g1v, scalar1=float(CAP), scalar2=None,
                            op0=ALU.is_lt)
            e.tensor_tensor(out=tmA[:, :], in0=tmA[:, :], in1=tmB[:, :], op=ALU.mult)
            e.tensor_tensor(out=slotv[:, :], in0=tmA[:, :], in1=g1v, op=ALU.mult)
            e.tensor_copy(out=tmC[:, :], in_=tmA[:, :])
            e.tensor_scalar(out=tmA[:, :], in0=i2v, scalar1=mye[:, 0:1], scalar2=None,
                            op0=ALU.is_equal)
            e.tensor_scalar(out=tmB[:, :], in0=g2v, scalar1=float(CAP), scalar2=None,
                            op0=ALU.is_lt)
            e.tensor_tensor(out=tmA[:, :], in0=tmA[:, :], in1=tmB[:, :], op=ALU.mult)
            e.tensor_tensor(out=tmB[:, :], in0=tmA[:, :], in1=g2v, op=ALU.mult)
            e.tensor_tensor(out=slotv[:, :], in0=slotv[:, :], in1=tmB[:, :], op=ALU.add)
            e.tensor_tensor(out=tmC[:, :], in0=tmC[:, :], in1=tmA[:, :], op=ALU.add)
            e.tensor_scalar(out=tmC[:, :], in0=tmC[:, :], scalar1=-float(CAP),
                            scalar2=float(CAP), op0=ALU.mult, op1=ALU.add)
            sV.inc(e.tensor_tensor(out=slotv[:, :], in0=slotv[:, :],
                                   in1=tmC[:, :], op=ALU.add))

        escope(_sc[0])
        _sc[0] = scope("P3d")
        p3dstack = ExitStack()
        ohb = p3dstack.enter_context(nc.sbuf_tensor("ohb", [128, 2 * 4 * 640], F32))
        svB = p3dstack.enter_context(nc.sbuf_tensor("svB", [128, 128], F32))
        drow = p3dstack.enter_context(nc.sbuf_tensor("drow", [1, 2560], F32))
        for ps in range(4):
            dpb = []

            @block.tensor
            def _(e):
                dpb.append(pool.acquire(e, 1))
                dpb.append(pool.acquire(e, 1))

            @BV
            def _(e, ps=ps):
                e.wait_ge(sV.h, sV.v)
                sV.inc(e.tensor_scalar(out=svB[:, :], in0=slotv[:, :],
                                       scalar1=1.0, scalar2=-640.0 * ps,
                                       op0=ALU.mult, op1=ALU.add))

            ohm = []
            mtm = []
            for gq in range(32):
                @block.vector
                def _(e, gq=gq):
                    if gq >= 2:
                        e.wait_ge(sT.h, mtm[gq - 2])
                    else:
                        e.wait_ge(sT.h, sT.v)
                        e.wait_ge(sV.h, sV.v)
                    for ci in range(4):
                        c = gq * 4 + ci
                        r = e.tensor_scalar(
                            out=ohb[:, ((gq % 2) * 4 + ci) * 640:((gq % 2) * 4 + ci) * 640 + 640],
                            in0=iota640[:, :], scalar1=svB[:, c:c + 1], scalar2=None,
                            op0=ALU.is_equal)
                    ohm.append(sV.inc(r))

                @block.tensor
                def _(e, gq=gq):
                    e.wait_ge(sV.h, ohm[gq])
                    for ci in range(4):
                        c = gq * 4 + ci
                        off = ((gq % 2) * 4 + ci) * 640
                        sT.inc(e.matmul(out=dpb[0][1][0:1, 0:512],
                                        lhsT=idsP[:, c:c + 1],
                                        rhs=ohb[:, off:off + 512],
                                        start=(c == 0), stop=(c == 127)))
                        sT.inc(e.matmul(out=dpb[1][1][0:1, 0:128],
                                        lhsT=idsP[:, c:c + 1],
                                        rhs=ohb[:, off + 512:off + 640],
                                        start=(c == 0), stop=(c == 127)))
                    mtm.append(sT.v)

            @BV
            def _(e, ps=ps):
                e.wait_ge(sT.h, sT.v)
                e.tensor_copy(out=drow[0:1, ps * 640:ps * 640 + 512],
                              in_=dpb[0][1][0:1, 0:512])
                v = sV.inc(e.tensor_copy(out=drow[0:1, ps * 640 + 512:ps * 640 + 640],
                                         in_=dpb[1][1][0:1, 0:128]))
                pool.release(dpb[0][0], 1, sV.h, v)
                pool.release(dpb[1][0], 1, sV.h, v)

        trm = []
        for j in range(20):
            tb = []

            @block.tensor
            def _(e, j=j):
                tb.append(pool.acquire(e, 1))
                if j == 0:
                    e.wait_ge(sV.h, sV.v)
                sT.inc(e.transpose(out=tb[0][1][0:128, 0:1],
                                   in_=drow[0:1, j * 128:(j + 1) * 128],
                                   identity=id32[0:1, 0:1]))

            @BV
            def _(e, j=j):
                e.wait_ge(sT.h, sT.v)
                v = sV.inc(e.tensor_copy(out=dispi[:, j:j + 1], in_=tb[0][1][0:128, 0:1]))
                pool.release(tb[0][0], 1, sV.h, v)

        p3dstack.close()

        barrier()
        if upto == "p3n":
            dbg_end()
            return nc

        escope(_sc[0])
        _sc[0] = scope("P4")
        with ExitStack() as p4:
            E4 = p4.enter_context
            wtg = E4(nc.sbuf_tensor("wtg", [128, 2 * E], BF16))
            wtu = E4(nc.sbuf_tensor("wtu", [128, 2 * E], BF16))
            wgbuf = E4(nc.sbuf_tensor("wgbuf", [128, 2 * E], BF16))
            wubuf = E4(nc.sbuf_tensor("wubuf", [128, 2 * E], BF16))
            wdbuf = E4(nc.sbuf_tensor("wdbuf", [128, 4 * E], BF16))
            wd67 = E4(nc.sbuf_tensor("wd67", [128, 64 * 256], BF16))
            hsl = E4(nc.sbuf_tensor("hsl", [128, 2 * T], BF16))
            hein = E4(nc.sbuf_tensor("hein4", [128, 4 * 512], BF16))
            gtmp = E4(nc.sbuf_tensor("gtmp", [128, 1024], F32))
            eoutF = E4(nc.sbuf_tensor("eoutF", [128, 8 * 512], BF16))
            x1sl = E4(nc.sbuf_tensor("x1sl", [128, 4 * E], F32))

            escope(_sc[0])
            _sc[0] = scope("P4g")
            tgm, tum, wcg, wcu, gum, sttm, hstok = [], [], [], [], [], [], []
            for m in range(64):
                ltok = [None]

                @block.gpsimd
                def _(e, m=m):
                    if m >= 2:
                        e.wait_ge(sT.h, tum[m - 2])
                    rWT.dma(e, m % 2, wtg[:, (m % 2) * E:(m % 2) * E + E],
                            ap["sgT"][m * 128:(m + 1) * 128, :])
                    ltok[0] = rWT.dma(e, m % 2, wtu[:, (m % 2) * E:(m % 2) * E + E],
                                      ap["suT"][m * 128:(m + 1) * 128, :])

                @block.tensor
                def _(e, m=m):
                    W(e, ltok[0])
                    if m == 0:
                        e.wait_ge(sV.h, sV.v)
                    else:
                        e.wait_ge(sWC.h, wcg[m - 1])
                    for k in range(KC):
                        mm = e.transpose(out=pb[:, k * 128:(k + 1) * 128],
                                         in_=wtg[:, (m % 2) * E + k * 128:(m % 2) * E + (k + 1) * 128],
                                         identity=idbf[:, :])
                    tgm.append(sT.inc(mm))
                    if m > 0:
                        e.wait_ge(sWC.h, wcu[m - 1])
                    for k in range(KC):
                        mm = e.transpose(out=pb2[:, k * 128:(k + 1) * 128],
                                         in_=wtu[:, (m % 2) * E + k * 128:(m % 2) * E + (k + 1) * 128],
                                         identity=idbf[:, :])
                    tum.append(sT.inc(mm))

                @block.vector
                def _(e, m=m):
                    e.wait_ge(sT.h, tgm[m])
                    if m >= 2:
                        e.wait_ge(sT.h, gum[m - 2])
                    wcg.append(sWC.inc(e.tensor_copy(
                        out=wgbuf[:, (m % 2) * E:(m % 2) * E + E], in_=pb[:, 0:E])))
                    e.wait_ge(sT.h, tum[m])
                    wcu.append(sWC.inc(e.tensor_copy(
                        out=wubuf[:, (m % 2) * E:(m % 2) * E + E], in_=pb2[:, 0:E])))

                prm = []
                for pr in range(2):
                    gub = []

                    @block.tensor
                    def _(e, m=m, pr=pr):
                        gub.append(pool.acquire(e, 2))
                        gub.append(pool.acquire(e, 2))
                        if pr == 0:
                            e.wait_ge(sWC.h, wcu[m])
                        for k in range(KC):
                            for hf in range(2):
                                c0 = pr * 1024 + hf * 512
                                sT.inc(e.matmul(
                                    out=gub[0][1][:, hf * 512:(hf + 1) * 512],
                                    lhsT=wgbuf[:, (m % 2) * E + k * 128:(m % 2) * E + (k + 1) * 128],
                                    rhs=fchunk(xfF, T)[:, k, c0:c0 + 512],
                                    start=(k == 0), stop=(k == KC - 1)))
                                sT.inc(e.matmul(
                                    out=gub[1][1][:, hf * 512:(hf + 1) * 512],
                                    lhsT=wubuf[:, (m % 2) * E + k * 128:(m % 2) * E + (k + 1) * 128],
                                    rhs=fchunk(xfF, T)[:, k, c0:c0 + 512],
                                    start=(k == 0), stop=(k == KC - 1)))
                        prm.append(sT.v)
                        if pr == 1:
                            gum.append(sT.v)

                    @BS
                    def _(e, m=m, pr=pr, mk=None):
                        e.wait_ge(sT.h, prm[pr])
                        sS.inc(e.activation(out=gtmp[:, :], in_=gub[0][1][:, 0:1024],
                                            func=AF.Silu))

                    @BV
                    def _(e, m=m, pr=pr):
                        e.wait_ge(sS.h, sS.v)
                        if m >= 2 and pr == 0:
                            W(e, hstok[m - 2])
                        v = sV.inc(e.scalar_tensor_tensor(
                            out=hsl[:, (m % 2) * T + pr * 1024:(m % 2) * T + pr * 1024 + 1024],
                            in0=gub[1][1][:, 0:1024], scalar=1.0, in1=gtmp[:, :],
                            op0=ALU.mult, op1=ALU.mult))
                        pool.release(gub[0][0], 2, sV.h, v)
                        pool.release(gub[1][0], 2, sV.h, v)
                        if pr == 1:
                            sttm.append(v)

                @block.sync
                def _(e, m=m):
                    e.wait_ge(sV.h, sttm[m])
                    t = None
                    for s in range(4):
                        t = rHS.dma(e, m % 2, HS[s][m * 128:(m + 1) * 128, :],
                                    hsl[:, (m % 2) * T + s * 512:(m % 2) * T + (s + 1) * 512])
                    hstok.append(t)

            wt_guard[0] = tum[-1]

            escope(_sc[0])
            _sc[0] = scope("P4d")
            wd_guard = [0]
            hi4_guard = [0]
            wd67m = []
            for sl in range(4):
                tok0 = sl * SLAB

                @block.sync
                def _(e, sl=sl, tok0=tok0):
                    if sl == 0:
                        for s_ in rHS.s:
                            e.wait_ge(s_.h, s_.v)
                    else:
                        e.wait_ge(sO.h, sO.v)
                    sX1.inc(e.dma_start(
                        out=x1sl[:, :].rearrange("p (a c) -> p a c", a=4),
                        in_=X1[tok0:tok0 + SLAB, :].rearrange("(a b) c -> b a c", b=128)), 16)
                x1tok = sX1.tok()

                dbanks = []

                @block.tensor
                def _(e):
                    for d in range(6):
                        dbanks.append(pool.acquire(e, 1))

                dm = []
                for m in range(64):
                    wtok = [None]
                    htok = [None]

                    @block.gpsimd
                    def _(e, m=m):
                        if m >= 4:
                            e.wait_ge(sT.h, dm[m - 4])
                        elif wd_guard[0]:
                            e.wait_ge(sT.h, wd_guard[0])
                        wtok[0] = rWD.dma(e, m % 4,
                                          wdbuf[:, (m % 4) * E:(m % 4) * E + E],
                                          ap["sd"][m * 128:(m + 1) * 128, :])

                    @block.sync
                    def _(e, m=m, sl=sl):
                        if m >= 4:
                            e.wait_ge(sT.h, dm[m - 4])
                        elif hi4_guard[0]:
                            e.wait_ge(sT.h, hi4_guard[0])
                        htok[0] = rHI.dma(e, m % 4,
                                          hein[:, (m % 4) * 512:(m % 4) * 512 + 512],
                                          HS[sl][m * 128:(m + 1) * 128, :])

                    @block.tensor
                    def _(e, m=m):
                        W(e, wtok[0])
                        W(e, htok[0])
                        if m == 0:
                            e.wait_ge(sV.h, sV.v)
                        for d in range(6):
                            sT.inc(e.matmul(
                                out=dbanks[d][1][:, 0:512],
                                lhsT=wdbuf[:, (m % 4) * E + d * 128:(m % 4) * E + (d + 1) * 128],
                                rhs=hein[:, (m % 4) * 512:(m % 4) * 512 + 512],
                                start=(m == 0), stop=(m == 63)))
                        dm.append(sT.v)

                    if sl == 0:
                        @block.vector
                        def _(e, m=m, wtok=wtok):
                            W(e, wtok[0])
                            wd67m.append(sWC.inc(e.tensor_copy(
                                out=wd67[:, m * 256:(m + 1) * 256],
                                in_=wdbuf[:, (m % 4) * E + 6 * 128:(m % 4) * E + 8 * 128])))

                wd_guard[0] = dm[-1]

                @BV
                def _(e):
                    e.wait_ge(sT.h, sT.v)
                    for d in range(6):
                        b0, pp = dbanks[d]
                        v = sV.inc(e.tensor_copy(out=eoutF[:, d * 512:(d + 1) * 512],
                                                 in_=pp[:, 0:512]))
                        pool.release(b0, 1, sV.h, v)

                d8b = []

                @block.tensor
                def _(e):
                    d8b.append(pool.acquire(e, 1))
                    d8b.append(pool.acquire(e, 1))

                dm2 = []
                for m in range(64):
                    htok = [None]

                    @block.sync
                    def _(e, m=m, sl=sl):
                        if m >= 4:
                            e.wait_ge(sT.h, dm2[m - 4])
                        else:
                            e.wait_ge(sT.h, dm[-1])
                        htok[0] = rHI.dma(e, m % 4,
                                          hein[:, (m % 4) * 512:(m % 4) * 512 + 512],
                                          HS[sl][m * 128:(m + 1) * 128, :])

                    @block.tensor
                    def _(e, m=m, sl=sl):
                        W(e, htok[0])
                        if m == 0 and sl == 0:
                            e.wait_ge(sWC.h, wd67m[-1])
                        for di in range(2):
                            sT.inc(e.matmul(
                                out=d8b[di][1][:, 0:512],
                                lhsT=wd67[:, m * 256 + di * 128:m * 256 + (di + 1) * 128],
                                rhs=hein[:, (m % 4) * 512:(m % 4) * 512 + 512],
                                start=(m == 0), stop=(m == 63)))
                        dm2.append(sT.v)
                hi4_guard[0] = dm2[-1]

                @BV
                def _(e):
                    e.wait_ge(sT.h, sT.v)
                    for di in range(2):
                        b0, pp = d8b[di]
                        v = sV.inc(e.tensor_copy(out=eoutF[:, (6 + di) * 512:(7 + di) * 512],
                                                 in_=pp[:, 0:512]))
                        pool.release(b0, 1, sV.h, v)

                for tg in range(4):
                    @block.tensor
                    def _(e, tg=tg):
                        e.wait_ge(sV.h, sV.v)
                        for d in range(8):
                            mm = e.transpose(out=pb[:, d * 128:(d + 1) * 128],
                                             in_=eoutF[:, d * 512 + tg * 128: d * 512 + (tg + 1) * 128],
                                             identity=idbf[:, :])
                        sT.inc(mm)

                    @BV
                    def _(e, tg=tg, x1tok=x1tok):
                        e.wait_ge(sT.h, sT.v)
                        W(e, x1tok)
                        sV.inc(e.tensor_tensor(out=x1sl[:, tg * E:(tg + 1) * E],
                                               in0=x1sl[:, tg * E:(tg + 1) * E],
                                               in1=pb[:, 0:E], op=ALU.add))

                @BS
                def _(e, tok0=tok0):
                    e.wait_ge(sV.h, sV.v)
                    sO.inc(e.dma_start(
                        out=X1[tok0:tok0 + SLAB, :].rearrange("(a b) c -> b a c", b=128),
                        in_=x1sl[:, :].rearrange("p (a c) -> p a c", a=4)), 16)
                    e.wait_ge(sO.h, sO.v)

        barrier()
        if upto == "p4n":
            dbg_end()
            return nc
        if upto == "p4":
            dump_dram_to_Y(X1)
            return nc

        escope(_sc[0])
        _sc[0] = scope("P5a")
        with ExitStack() as p5:
            E5 = p5.enter_context
            xgF = E5(nc.sbuf_tensor("xgF", [128, KC * CAP], BF16))
            xgt = E5(nc.sbuf_tensor("xgt", [128, 4 * E], BF16))
            wgbuf = E5(nc.sbuf_tensor("wgbuf5", [128, 2 * E], BF16))
            wubuf = E5(nc.sbuf_tensor("wubuf5", [128, 2 * E], BF16))
            wtg5 = E5(nc.sbuf_tensor("wtg5", [128, 2 * E], BF16))
            wtu5 = E5(nc.sbuf_tensor("wtu5", [128, 2 * E], BF16))
            gtmp = E5(nc.sbuf_tensor("gtmp5", [128, 512], BF16))
            hebuf = E5(nc.sbuf_tensor("hebuf", [128, 2 * CAP], BF16))

            xgm = []
            for j in range(20):
                gtok = [None]

                @block.gpsimd
                def _(e, j=j):
                    if j == 0:
                        e.wait_ge(sC.h, sC.v)
                        e.wait_ge(sV.h, sV.v)
                    if j >= 4:
                        e.wait_ge(sT.h, xgm[j - 4])
                    sm = rXG.s[j % 4]
                    sm.inc(e.indirect_dma_start(
                        xgt[:, (j % 4) * E:(j % 4) * E + E], None,
                        xf_all.ap(), IndirectOffsetOnAxis(dispi[:, j:j + 1], 0)), 16)
                    gtok[0] = sm.tok()

                @block.tensor
                def _(e, j=j):
                    W(e, gtok[0])
                    e.wait_ge(sV.h, sV.v)
                    for fc in range(KC):
                        mm = e.transpose(out=pb[:, fc * 128:(fc + 1) * 128],
                                         in_=xgt[:, (j % 4) * E + fc * 128:(j % 4) * E + (fc + 1) * 128],
                                         identity=idbf[:, :])
                    xgm.append(sT.inc(mm))

                @BV
                def _(e, j=j):
                    e.wait_ge(sT.h, sT.v)
                    sV.inc(e.tensor_copy(
                        out=xgF[:, :].rearrange("p (a b c) -> p a b c", a=KC, b=20)[:, :, j, :],
                        in_=pb[:, :].rearrange("p (a c) -> p a c", a=KC)))

            escope(_sc[0])
            _sc[0] = scope("P5au")
            hem = []
            tgm5, tum5, wcg5, wcu5, gum5 = [], [], [], [], []
            for m in range(32):
                ltok = [None]

                @block.gpsimd
                def _(e, m=m):
                    if m >= 2:
                        e.wait_ge(sT.h, tum5[m - 2])
                        e.wait_ge(sT.h, hem[m - 2][0])
                    elif wt_guard[0]:
                        e.wait_ge(sT.h, wt_guard[0])
                    rWT.dma(e, m % 2, wtg5[:, (m % 2) * E:(m % 2) * E + E],
                            ap["rgT"][m * 128:(m + 1) * 128, :])
                    ltok[0] = rWT.dma(e, m % 2, wtu5[:, (m % 2) * E:(m % 2) * E + E],
                                      ap["ruT"][m * 128:(m + 1) * 128, :])

                @block.tensor
                def _(e, m=m):
                    W(e, ltok[0])
                    if m == 0:
                        e.wait_ge(sV.h, sV.v)
                        e.wait_ge(sWC.h, sWC.v)
                    else:
                        e.wait_ge(sWC.h, wcg5[m - 1])
                    for k in range(KC):
                        mm = e.transpose(out=pb[:, k * 128:(k + 1) * 128],
                                         in_=wtg5[:, (m % 2) * E + k * 128:(m % 2) * E + (k + 1) * 128],
                                         identity=idbf[:, :])
                    tgm5.append(sT.inc(mm))
                    if m > 0:
                        e.wait_ge(sWC.h, wcu5[m - 1])
                    for k in range(KC):
                        mm = e.transpose(out=pb2[:, k * 128:(k + 1) * 128],
                                         in_=wtu5[:, (m % 2) * E + k * 128:(m % 2) * E + (k + 1) * 128],
                                         identity=idbf[:, :])
                    tum5.append(sT.inc(mm))

                @block.vector
                def _(e, m=m):
                    e.wait_ge(sT.h, tgm5[m])
                    if m >= 2:
                        e.wait_ge(sT.h, gum5[m - 2])
                    wcg5.append(sWC.inc(e.tensor_copy(
                        out=wgbuf[:, (m % 2) * E:(m % 2) * E + E], in_=pb[:, 0:E])))
                    e.wait_ge(sT.h, tum5[m])
                    wcu5.append(sWC.inc(e.tensor_copy(
                        out=wubuf[:, (m % 2) * E:(m % 2) * E + E], in_=pb2[:, 0:E])))

                subm = []
                for sub in range(5):
                    gub = []

                    @block.tensor
                    def _(e, m=m, sub=sub):
                        gub.append(pool.acquire(e, 1))
                        gub.append(pool.acquire(e, 1))
                        if sub == 0:
                            e.wait_ge(sWC.h, wcu5[m])
                        for k in range(KC):
                            sT.inc(e.matmul(out=gub[0][1][:, 0:512],
                                            lhsT=wgbuf[:, (m % 2) * E + k * 128:(m % 2) * E + (k + 1) * 128],
                                            rhs=fchunk(xgF, CAP)[:, k, sub * 512:(sub + 1) * 512],
                                            start=(k == 0), stop=(k == KC - 1)))
                            sT.inc(e.matmul(out=gub[1][1][:, 0:512],
                                            lhsT=wubuf[:, (m % 2) * E + k * 128:(m % 2) * E + (k + 1) * 128],
                                            rhs=fchunk(xgF, CAP)[:, k, sub * 512:(sub + 1) * 512],
                                            start=(k == 0), stop=(k == KC - 1)))
                        subm.append(sT.v)

                    @BS
                    def _(e, sub=sub):
                        e.wait_ge(sT.h, subm[sub])
                        sS.inc(e.activation(out=gtmp[:, :], in_=gub[0][1][:, 0:512],
                                            func=AF.Silu))

                    @BV
                    def _(e, m=m, sub=sub):
                        e.wait_ge(sS.h, sS.v)
                        if m >= 2 and sub == 0:
                            W(e, hem[m - 2][1])
                        v = sV.inc(e.scalar_tensor_tensor(
                            out=hebuf[:, (m % 2) * CAP + sub * 512:(m % 2) * CAP + (sub + 1) * 512],
                            in0=gub[1][1][:, 0:512], scalar=1.0, in1=gtmp[:, :],
                            op0=ALU.mult, op1=ALU.mult))
                        pool.release(gub[0][0], 1, sV.h, v)
                        pool.release(gub[1][0], 1, sV.h, v)

                gum5.append(sT.v)
                vlast = sV.v

                @block.gpsimd
                def _(e, m=m, vlast=vlast):
                    e.wait_ge(sV.h, vlast)
                    hetok = rHE.dma(e, m % 2, HE[m * 128:(m + 1) * 128, :],
                                    hebuf[:, (m % 2) * CAP:(m % 2) * CAP + CAP])
                    hem.append((sT.v, hetok))

        barrier()
        if upto == "p5an":
            dbg_end()
            return nc

        escope(_sc[0])
        _sc[0] = scope("P5b")
        with ExitStack() as p5b:
            E5 = p5b.enter_context
            rdsb = E5(nc.sbuf_tensor("rdsb", [128, 32 * E], BF16))
            hein = E5(nc.sbuf_tensor("hein", [128, 4 * 512], BF16))
            eoutF = E5(nc.sbuf_tensor("eoutF5", [128, 8 * 512], BF16))
            erow = E5(nc.sbuf_tensor("erow", [128, E], BF16))

            @block.gpsimd
            def _(e):
                for m in range(32):
                    sRD.inc(e.dma_start(out=rdsb[:, m * E:(m + 1) * E],
                                        in_=ap["rd"][m * 128:(m + 1) * 128, :]), 16)
                sG.inc(e.dma_start(out=rout_in[CAP:CAP1, :], in_=zbf[:, :]), 16)
                e.wait_ge(sG.h, sG.v)
            rdtok = sRD.tok()

            ertoks = []
            hi_guard = [0]
            for sub in range(5):
                dbanks = []

                @block.tensor
                def _(e, rdtok=rdtok):
                    W(e, rdtok)
                    for d in range(6):
                        dbanks.append(pool.acquire(e, 1))

                dm = []
                for m in range(32):
                    htok = [None]

                    @block.sync
                    def _(e, m=m, sub=sub):
                        if m >= 4:
                            e.wait_ge(sT.h, dm[m - 4])
                        elif hi_guard[0]:
                            e.wait_ge(sT.h, hi_guard[0])
                        htok[0] = rHI.dma(e, m % 4,
                                          hein[:, (m % 4) * 512:(m % 4) * 512 + 512],
                                          HE[m * 128:(m + 1) * 128, sub * 512:(sub + 1) * 512])

                    @block.tensor
                    def _(e, m=m):
                        W(e, htok[0])
                        for d in range(6):
                            sT.inc(e.matmul(out=dbanks[d][1][:, 0:512],
                                            lhsT=rdsb[:, m * E + d * 128: m * E + (d + 1) * 128],
                                            rhs=hein[:, (m % 4) * 512:(m % 4) * 512 + 512],
                                            start=(m == 0), stop=(m == 31)))
                        dm.append(sT.v)
                hi_guard[0] = dm[-1]

                @BV
                def _(e):
                    e.wait_ge(sT.h, sT.v)
                    for d in range(6):
                        b0, pp = dbanks[d]
                        v = sV.inc(e.tensor_copy(out=eoutF[:, d * 512:(d + 1) * 512],
                                                 in_=pp[:, 0:512]))
                        pool.release(b0, 1, sV.h, v)

                d8b = []

                @block.tensor
                def _(e):
                    d8b.append(pool.acquire(e, 1))
                    d8b.append(pool.acquire(e, 1))

                dm2 = []
                for m in range(32):
                    htok = [None]

                    @block.sync
                    def _(e, m=m, sub=sub):
                        if m >= 4:
                            e.wait_ge(sT.h, dm2[m - 4])
                        elif hi_guard[0]:
                            e.wait_ge(sT.h, hi_guard[0])
                        htok[0] = rHI.dma(e, m % 4,
                                          hein[:, (m % 4) * 512:(m % 4) * 512 + 512],
                                          HE[m * 128:(m + 1) * 128, sub * 512:(sub + 1) * 512])

                    @block.tensor
                    def _(e, m=m):
                        W(e, htok[0])
                        for di in range(2):
                            sT.inc(e.matmul(out=d8b[di][1][:, 0:512],
                                            lhsT=rdsb[:, m * E + (6 + di) * 128: m * E + (7 + di) * 128],
                                            rhs=hein[:, (m % 4) * 512:(m % 4) * 512 + 512],
                                            start=(m == 0), stop=(m == 31)))
                        dm2.append(sT.v)
                hi_guard[0] = dm2[-1]

                @BV
                def _(e):
                    e.wait_ge(sT.h, sT.v)
                    for di in range(2):
                        b0, pp = d8b[di]
                        v = sV.inc(e.tensor_copy(out=eoutF[:, (6 + di) * 512:(7 + di) * 512],
                                                 in_=pp[:, 0:512]))
                        pool.release(b0, 1, sV.h, v)

                for tg in range(4):
                    @block.tensor
                    def _(e, tg=tg):
                        e.wait_ge(sV.h, sV.v)
                        for d in range(8):
                            mm = e.transpose(out=pb[:, d * 128:(d + 1) * 128],
                                             in_=eoutF[:, d * 512 + tg * 128: d * 512 + (tg + 1) * 128],
                                             identity=idbf[:, :])
                        sT.inc(mm)

                    @BV
                    def _(e, tg=tg):
                        e.wait_ge(sT.h, sT.v)
                        if ertoks:
                            W(e, ertoks[-1])
                        sV.inc(e.tensor_copy(out=erow[:, :], in_=pb[:, 0:E]))

                    @block.gpsimd
                    def _(e, sub=sub, tg=tg):
                        e.wait_ge(sV.h, sV.v)
                        ertoks.append(rER.dma(
                            e, 0,
                            rout_in[sub * 512 + tg * 128: sub * 512 + (tg + 1) * 128, :],
                            erow[:, :]))

            @block.gpsimd
            def _(e):
                W(e, ertoks[-1])
                sC.inc(e.collective_compute(
                    "AllGather", ALU.bypass, replica_groups=[list(range(NCORE))],
                    ins=[rout_in.ap().opt()], outs=[rout_all.ap().opt()]))

        barrier()
        if upto == "p5n":
            dbg_end()
            return nc
        if upto == "p5":
            dump_dram_to_Y(X1)
            return nc

        escope(_sc[0])
        _sc[0] = scope("P6")
        with ExitStack() as p6:
            E6 = p6.enter_context
            r1b = E6(nc.sbuf_tensor("r1b", [128, 2 * E], BF16))
            r2b = E6(nc.sbuf_tensor("r2b", [128, 2 * E], BF16))
            x1in = E6(nc.sbuf_tensor("x1in", [128, 2 * E], F32))
            outb = E6(nc.sbuf_tensor("outb", [128, 2 * E], F32))

            om = []
            stom = []
            for t in range(NT):
                rtok = [None]

                @block.gpsimd
                def _(e, t=t):
                    if t == 0:
                        e.wait_ge(sC.h, sC.v)
                    if t >= 2:
                        e.wait_ge(sV.h, om[t - 2])
                    sm = rRR.s[t % 2]
                    sm.inc(e.indirect_dma_start(
                        r1b[:, (t % 2) * E:(t % 2) * E + E], None,
                        rout_all.ap(), IndirectOffsetOnAxis(loca[:, t:t + 1], 0)), 16)
                    sm.inc(e.indirect_dma_start(
                        r2b[:, (t % 2) * E:(t % 2) * E + E], None,
                        rout_all.ap(), IndirectOffsetOnAxis(loca[:, 16 + t:16 + t + 1], 0)), 16)
                    rtok[0] = sm.tok()

                xtok = [None]

                @block.sync
                def _(e, t=t):
                    if t >= 2:
                        e.wait_ge(sV.h, om[t - 2])
                    xtok[0] = rXI.dma(e, t % 2,
                                      x1in[:, (t % 2) * E:(t % 2) * E + E],
                                      X1.rearrange("(a b) c -> a b c", b=128)[t])

                @BV
                def _(e, t=t):
                    W(e, rtok[0])
                    W(e, xtok[0])
                    if t >= 2:
                        e.wait_ge(sO.h, stom[t - 2])
                    e.scalar_tensor_tensor(out=outb[:, (t % 2) * E:(t % 2) * E + E],
                                           in0=r1b[:, (t % 2) * E:(t % 2) * E + E],
                                           scalar=st(W1, t, 1),
                                           in1=x1in[:, (t % 2) * E:(t % 2) * E + E],
                                           op0=ALU.mult, op1=ALU.add)
                    om.append(sV.inc(e.scalar_tensor_tensor(
                        out=outb[:, (t % 2) * E:(t % 2) * E + E],
                        in0=r2b[:, (t % 2) * E:(t % 2) * E + E],
                        scalar=st(W2, t, 1),
                        in1=outb[:, (t % 2) * E:(t % 2) * E + E],
                        op0=ALU.mult, op1=ALU.add)))

                @BS
                def _(e, t=t):
                    e.wait_ge(sV.h, om[t])
                    sO.inc(e.dma_start(out=Y.rearrange("(a b) c -> a b c", b=128)[t],
                                       in_=outb[:, (t % 2) * E:(t % 2) * E + E]), 16)
                    e.wait_ge(sO.h, sO.v)
                    stom.append(sO.v)

            @BS
            def _(e):
                e.wait_ge(sO.h, sO.v)

        escope(_sc[0])

    return nc


_NC_CACHE = None


def kernel(**inputs):
    global _NC_CACHE
    f32 = np.float32
    x = np.ascontiguousarray(np.asarray(inputs["L0_attn_x"], f32))
    gamma = np.asarray(inputs["L0_attn_gamma"], f32)
    beta = np.asarray(inputs["L0_attn_beta"], f32)
    shared = dict(
        Wq=np.ascontiguousarray(np.asarray(inputs["L0_attn_Wq"], f32)),
        Wk=np.ascontiguousarray(np.asarray(inputs["L0_attn_Wk"], f32)),
        Wv=np.ascontiguousarray(np.asarray(inputs["L0_attn_Wv"], f32)),
        Wo=np.ascontiguousarray(np.asarray(inputs["L0_attn_Wo"], f32)),
        Wg1=np.ascontiguousarray(np.asarray(inputs["L0_attn_Wg1"], f32)),
        Wg2=np.ascontiguousarray(np.asarray(inputs["L0_attn_Wg2"], f32)),
        gammaB=np.ascontiguousarray(np.broadcast_to(gamma[None, :], (128, E))),
        betaB=np.ascontiguousarray(np.broadcast_to(beta[None, :], (128, E))),
        rwT=np.ascontiguousarray(np.asarray(inputs["L0_ffn_router_w"], f32).T),
        sgT=np.ascontiguousarray(np.asarray(inputs["L0_ffn_sg"], f32).T),
        suT=np.ascontiguousarray(np.asarray(inputs["L0_ffn_su"], f32).T),
        sd=np.ascontiguousarray(np.asarray(inputs["L0_ffn_sd"], f32)),
    )
    if _NC_CACHE is None:
        _NC_CACHE = build_nc()
    nc = _NC_CACHE
    in_maps = []
    for c in range(NCORE):
        im = dict(shared)
        im.update({k: np.ascontiguousarray(v) for k, v in _consts(c).items()})
        im["X"] = np.ascontiguousarray(x[c])
        im["rgT"] = np.ascontiguousarray(np.asarray(inputs["L0_ffn_rg"][c], f32).T)
        im["ruT"] = np.ascontiguousarray(np.asarray(inputs["L0_ffn_ru"][c], f32).T)
        im["rd"] = np.ascontiguousarray(np.asarray(inputs["L0_ffn_rd"][c], f32))
        in_maps.append(im)
    res = run_bass_kernel_spmd(nc, in_maps, core_ids=list(range(NCORE)))
    return np.stack([res.results[c]["Y"] for c in range(NCORE)], axis=0)


# version 11
# speedup vs baseline: 1.0762x; 1.0762x over previous
import numpy as np
import ml_dtypes
from contextlib import ExitStack

import concourse.bass as bass
import concourse.mybir as mybir
from concourse.bass_core_rust import IndirectOffsetOnAxis
from concourse.bass_utils import run_bass_kernel_spmd

F32 = mybir.dt.float32
BF16 = mybir.dt.bfloat16
I32 = mybir.dt.int32
AF = mybir.ActivationFunctionType
ALU = mybir.AluOpType
AXX = mybir.AxisListType.X

NCORE = 8
T = 2048
NT = 16
E = 1024
KC = 8
H = 16
D = 64
CH = 128
G = NCORE * T
CAP = 2560
CAP1 = CAP + 1
EPS_LN = 1e-5
EPS_RMS = float(np.finfo(np.float32).eps)
BIG = 1024.0
IS = 8192
IR = 4096
SLAB = 512


def _slopes():
    def po2(m):
        start = 2.0 ** (-(2.0 ** (-(np.log2(m) - 3))))
        return [start * start ** i for i in range(m)]
    s = np.array(po2(H), dtype=np.float32)
    return (s * np.float32(1.0 - 0.0 / 11.0 + 1e-5)).astype(np.float32)


def _consts(core):
    s = _slopes()
    idx = np.arange(CH, dtype=np.float32)
    maskT = np.zeros((CH, H * CH), np.float32)
    for h in range(H):
        d = idx[:, None] - idx[None, :]
        m = np.where(d >= 0, np.exp(-s[h] * np.maximum(d, 0.0)), 0.0).astype(np.float32)
        maskT[:, h * CH:(h + 1) * CH] = m.T
    lam_q = np.exp(-s[:, None] * (idx[None, :] + 1.0)).astype(np.float32)
    lam_k = np.exp(-s[:, None] * (CH - 1.0 - idx[None, :])).astype(np.float32)
    lamqF = np.zeros((E, CH), np.float32)
    for r in range(E):
        lamqF[r] = lam_q[r // D]
    lamkT = np.zeros((CH, E), np.float32)
    for f in range(E):
        lamkT[:, f] = lam_k[f // D]
    pfx = (np.arange(128)[:, None] < np.arange(128)[None, :]).astype(np.float32)
    iden = np.eye(128, dtype=np.float32)
    iota8 = np.tile(np.arange(8, dtype=np.float32)[None, :], (128, 1))
    bmi = BIG - iota8
    mye = np.full((128, 1), float(core), np.float32)
    prefm = np.zeros((8, 128), np.float32)
    prefm[:core, :] = 1.0
    idsP = (np.arange(128, dtype=np.float32)[:, None] * 128.0
            + np.arange(128, dtype=np.float32)[None, :])
    ones_row = np.ones((1, 128), np.float32)
    ones_col = np.ones((128, 1), np.float32)
    iota640 = np.ascontiguousarray(np.broadcast_to(
        np.arange(640, dtype=np.float32)[None, :], (128, 640)))
    return dict(C_maskT=maskT, C_lamqF=lamqF, C_lamkT=lamkT, C_pfx=pfx,
                C_id=iden, C_iota8=iota8, C_bmi=bmi, C_mye=mye,
                C_prefm=prefm, C_idsP=idsP, C_onesr=ones_row, C_onesc=ones_col,
                C_iota640=iota640)


def build_nc(upto=None):
    nc = bass.Bass("TRN2", target_bir_lowering=False, debug=False,
                   detect_race_conditions=False)
    ap = {}

    def din(name, shape, dt=F32):
        ap[name] = nc.dram_tensor(name, shape, dt, kind="ExternalInput").ap()

    din("X", [T, E])
    din("Wq", [E, E]); din("Wk", [E, E]); din("Wv", [E, E]); din("Wo", [E, E])
    din("Wg1", [E, D]); din("Wg2", [D, E])
    din("gammaB", [128, E]); din("betaB", [128, E])
    din("rwT", [E, 8])
    din("rgT", [IR, E]); din("ruT", [IR, E]); din("rd", [IR, E])
    din("sgT", [IS, E]); din("suT", [IS, E]); din("sd", [IS, E])
    din("C_maskT", [CH, H * CH]); din("C_lamqF", [E, CH]); din("C_lamkT", [CH, E])
    din("C_pfx", [128, 128]); din("C_id", [128, 128]); din("C_iota8", [128, 8])
    din("C_bmi", [128, 8]); din("C_mye", [128, 1]); din("C_prefm", [8, 128])
    din("C_idsP", [128, 128]); din("C_onesr", [1, 128]); din("C_onesc", [128, 1])
    din("C_iota640", [128, 640])
    Y = nc.dram_tensor("Y", [T, E], F32, kind="ExternalOutput").ap()

    X1 = nc.dram_tensor("X1", [T, E], F32).ap()
    HS = [nc.dram_tensor(f"HS{s}", [IS, 512], BF16).ap() for s in range(4)]
    xf_in = nc.dram_tensor("xf_in", [T, E], BF16)
    xf_all = nc.dram_tensor("xf_all", [G, E], BF16)
    cnt_in = nc.dram_tensor("cnt_in", [1, 8], F32)
    cnt_all = nc.dram_tensor("cnt_all", [8, 8], F32)
    meta_in = nc.dram_tensor("meta_in", [T, 4], F32)
    meta_all = nc.dram_tensor("meta_all", [G, 4], F32)
    disp = nc.dram_tensor("disp", [2688, 1], F32)
    HE = nc.dram_tensor("HE", [IR, CAP], BF16)
    rout_in = nc.dram_tensor("rout_in", [CAP1, E], BF16)
    rout_all = nc.dram_tensor("rout_all", [NCORE * CAP1, E], BF16)

    lam_c = np.exp(-_slopes() * CH).astype(np.float32)

    with ExitStack() as ctx:
        EC = ctx.enter_context

        xn = EC(nc.sbuf_tensor("xn", [128, NT * E], BF16))
        maskT = EC(nc.sbuf_tensor("maskT", [128, H * CH], BF16))
        lamqF = EC(nc.sbuf_tensor("lamqF", [128, KC * CH], BF16))
        lamkT = EC(nc.sbuf_tensor("lamkT", [128, E], BF16))
        pfx = EC(nc.sbuf_tensor("pfx", [128, 128], F32))
        id32 = EC(nc.sbuf_tensor("id32", [128, 128], F32))
        idbf = EC(nc.sbuf_tensor("idbf", [128, 128], BF16))
        iota8 = EC(nc.sbuf_tensor("iota8", [128, 8], F32))
        iota640 = EC(nc.sbuf_tensor("iota640", [128, 640], F32))
        bmi = EC(nc.sbuf_tensor("bmi", [128, 8], F32))
        mye = EC(nc.sbuf_tensor("mye", [128, 1], F32))
        prefm = EC(nc.sbuf_tensor("prefm", [8, 128], F32))
        idsP = EC(nc.sbuf_tensor("idsP", [128, 128], F32))
        onesr = EC(nc.sbuf_tensor("onesr", [1, 128], F32))
        onesc = EC(nc.sbuf_tensor("onesc", [128, 1], F32))
        gammaB = EC(nc.sbuf_tensor("gammaB_s", [128, E], BF16))
        betaB = EC(nc.sbuf_tensor("betaB_s", [128, E], BF16))
        rwTs = EC(nc.sbuf_tensor("rwTs", [128, KC * 8], BF16))
        zer = EC(nc.sbuf_tensor("zer", [128, 64], F32))
        zbf = EC(nc.sbuf_tensor("zbf", [1, E], BF16))
        stat = EC(nc.sbuf_tensor("stat", [128, 16 * 8], F32))
        lg = EC(nc.sbuf_tensor("lg", [128, 128], F32))
        m8a = EC(nc.sbuf_tensor("m8a", [128, 128], F32))
        Ma = EC(nc.sbuf_tensor("Ma", [128, 128], F32))
        ranka = EC(nc.sbuf_tensor("ranka", [128, 128], F32))
        i1a = EC(nc.sbuf_tensor("i1a", [128, 16], F32))
        i2a = EC(nc.sbuf_tensor("i2a", [128, 16], F32))
        loca = EC(nc.sbuf_tensor("loca", [128, 32], I32))
        meta = EC(nc.sbuf_tensor("meta", [128, 64], F32))
        Rrow = EC(nc.sbuf_tensor("Rrow", [1, 8], F32))
        cnt8 = EC(nc.sbuf_tensor("cnt8", [8, 8], F32))
        Osb = EC(nc.sbuf_tensor("Osb", [128, 8], F32))
        dispi = EC(nc.sbuf_tensor("dispi", [128, 20], I32))
        t8a = EC(nc.sbuf_tensor("t8a", [128, 8], F32))
        t8b = EC(nc.sbuf_tensor("t8b", [128, 8], F32))
        t1a = EC(nc.sbuf_tensor("t1a", [128, 1], F32))
        t1b = EC(nc.sbuf_tensor("t1b", [128, 1], F32))
        epsln = EC(nc.sbuf_tensor("epsln", [128, 1], F32))
        epsrm = EC(nc.sbuf_tensor("epsrm", [128, 1], F32))
        sqs = EC(nc.sbuf_tensor("sqs", [128, E], F32))

        pf = EC(nc.psum_tensor("pf", [128, 3072], F32))
        pb = EC(nc.psum_tensor("pb", [128, 1024], BF16))
        pb2 = EC(nc.psum_tensor("pb2", [128, 1024], BF16))

        SEMS = []

        class Sem:
            def __init__(self, name):
                self.h = EC(nc.semaphore(name))
                self.v = 0
                SEMS.append(self)

            def inc(self, instr, k=1):
                instr.then_inc(self.h, k)
                self.v += k
                return self.v

            def tok(self):
                return (self.h, self.v)

        class Ring:
            def __init__(self, name, nslots):
                self.s = [Sem(f"{name}{i}") for i in range(nslots)]

            def dma(self, eng, slot, out, in_, **kw):
                sm = self.s[slot]
                sm.inc(eng.dma_start(out=out, in_=in_, **kw), 16)
                return sm.tok()

        def W(e, tok):
            e.wait_ge(tok[0], tok[1])

        sY = Sem("sY")
        sG = Sem("sG")
        sC = Sem("sC")
        sO = Sem("sO")
        sT = Sem("sT")
        sV = Sem("sV")
        sS = Sem("sS")
        sWg = Sem("sWg")
        sX0 = Sem("sX0")
        sX1 = Sem("sX1")
        sSC = Sem("sSC")
        sRD = Sem("sRD")
        rLn = Ring("rLn", 2)
        rWb = Ring("rWb", 2)
        rGU = Ring("rGU", 2)
        rWD = Ring("rWD", 4)
        rWD2 = Ring("rWD2", 2)
        rHE = Ring("rHE", 2)
        rHI = Ring("rHI", 4)
        rXG = Ring("rXG", 4)
        rRR = Ring("rRR", 2)
        rXI = Ring("rXI", 2)
        rER = Ring("rER", 1)
        rWT = Ring("rWT", 2)
        rHS = Ring("rHS", 2)
        sWC = Sem("sWC")
        sP = Sem("sP")
        wt_guard = [0]

        class PsumPool:
            def __init__(self, pf, nbank=6):
                self.pf = pf
                self.n = nbank
                self.busy = [False] * nbank
                self.free_at = [None] * nbank

            def acquire(self, eng, nb=1):
                for b0 in range(self.n - nb + 1):
                    if not any(self.busy[b0:b0 + nb]):
                        for b in range(b0, b0 + nb):
                            fa = self.free_at[b]
                            if fa is not None:
                                eng.wait_ge(fa[0], fa[1])
                                self.free_at[b] = None
                            self.busy[b] = True
                        return b0, self.pf[:, b0 * 512:(b0 + nb) * 512]
                raise RuntimeError("psum pool exhausted")

            def release(self, b0, nb, sem, val):
                for b in range(b0, b0 + nb):
                    self.busy[b] = False
                    self.free_at[b] = (sem, val)

        pool = PsumPool(pf)
        block = EC(nc.Block())

        _SYNC_OPS = frozenset((
            "tensor_scalar", "tensor_tensor", "tensor_add", "tensor_sub",
            "tensor_mul", "tensor_max", "tensor_copy", "tensor_scalar_add",
            "tensor_scalar_sub", "tensor_scalar_mul", "tensor_scalar_max",
            "tensor_scalar_min", "tensor_single_scalar", "scalar_tensor_tensor",
            "reduce_sum", "reduce_max", "tensor_reduce", "reciprocal", "max",
            "memset", "memzero", "select", "activation", "copy", "mul", "add",
            "square", "sqrt", "tensor_relu", "copy_predicated"))

        _LASTW = {"v": 0, "s": 0}

        class SyncedEng:
            def __init__(self, e, tag):
                self._e = e
                self._tag = tag

            def __getattr__(self, name):
                f = getattr(self._e, name)
                if name not in _SYNC_OPS:
                    return f

                def g(*a, **k):
                    if _LASTW[self._tag] < 256:
                        self._e.drain()
                    try:
                        w = int(k["out"].shape[-1])
                    except Exception:
                        w = 0
                    if "accum_out" in k and k["accum_out"] is not None:
                        try:
                            w = min(w, int(k["accum_out"].shape[-1]))
                        except Exception:
                            w = 0
                    _LASTW[self._tag] = w
                    return f(*a, **k)
                return g

        def BV(f):
            @block.vector
            def _(e):
                f(SyncedEng(e, "v"))

        def BS(f):
            @block.scalar
            def _(e):
                f(SyncedEng(e, "s"))

        def barrier():
            vals = [(s, s.v) for s in SEMS if s.v > 0]

            def emit(eng, own):
                for s, v in vals:
                    if s is not own:
                        eng.wait_ge(s.h, v)

            @block.sync
            def _(e):
                emit(e, sY)

            @block.gpsimd
            def _(e):
                emit(e, sG)

            @block.tensor
            def _(e):
                emit(e, sT)

            @BV
            def _(e):
                emit(e, sV)

            @BS
            def _(e):
                emit(e, sS)

        SUM, SQ, MU, RS, VAR, RSR, W1, W2 = range(8)

        def st(col, t0=0, n=16):
            return stat[:, col * 16 + t0: col * 16 + t0 + n]

        def fchunk(buf, width):
            return buf[:, :].rearrange("p (a c) -> p a c", c=width)

        _sqs_mark = [0]

        def sqs_square(e, in_ap, accum_ap):
            if _sqs_mark[0]:
                e.wait_ge(sS.h, _sqs_mark[0])
            r = sS.inc(e.activation(out=sqs[:, :], in_=in_ap, func=AF.Square,
                                    accum_out=accum_ap))
            _sqs_mark[0] = r
            return r

        def scope(name):
            sid, _ = nc.enter_named_scope(name, False)
            return (name, sid)

        def escope(s):
            nc.leave_named_scope(s[0], s[1], False)

        _sc = [scope("P01")]
        @block.sync
        def _(e):
            for n, d in (("C_pfx", pfx), ("C_id", id32), ("C_iota8", iota8),
                         ("C_bmi", bmi), ("C_mye", mye), ("C_prefm", prefm),
                         ("C_idsP", idsP), ("C_onesr", onesr), ("C_onesc", onesc),
                         ("C_iota640", iota640)):
                sY.inc(e.dma_start(out=d[:, :], in_=ap[n]), 16)
            e.wait_ge(sY.h, sY.v)

        @block.gpsimd
        def _(e):
            sG.inc(e.dma_start(out=maskT[:, :], in_=ap["C_maskT"]), 16)
            sG.inc(e.dma_start(out=lamqF[:, :],
                               in_=ap["C_lamqF"].rearrange("(a b) c -> b a c", b=128)), 16)
            sG.inc(e.dma_start(out=lamkT[:, :], in_=ap["C_lamkT"]), 16)
            sG.inc(e.dma_start(out=gammaB[:, :], in_=ap["gammaB"]), 16)
            sG.inc(e.dma_start(out=betaB[:, :], in_=ap["betaB"]), 16)
            sG.inc(e.dma_start(out=rwTs[:, :],
                               in_=ap["rwT"].rearrange("(a b) c -> b a c", b=128)), 16)
            sG.inc(e.dma_start(out=idbf[:, :], in_=ap["C_id"]), 16)
            e.wait_ge(sG.h, sG.v)

        @BV
        def _(e):
            e.memset(zer[:, :], 0.0)
            e.memset(epsln[:, :], EPS_LN)
            e.memset(epsrm[:, :], EPS_RMS)
            sV.inc(e.memset(zbf[:, :], 0.0))

        barrier()

        _ln_n = [0]

        def emit_ln(src_dram, out_sb, do_rms):
            _ln_n[0] += 1
            lnctx = ExitStack()
            lnin = lnctx.enter_context(
                nc.sbuf_tensor("lnin%d" % _ln_n[0], [128, 2 * E], F32))
            vm, sm = [], []
            toks = []
            for t in range(NT):
                @block.sync
                def _(e, t=t):
                    if t >= 2:
                        e.wait_ge(sV.h, vm[t - 2])
                        e.wait_ge(sS.h, sm[t - 2])
                    toks.append(rLn.dma(e, t % 2,
                                        lnin[:, (t % 2) * E:(t % 2) * E + E],
                                        src_dram.rearrange("(a b) c -> a b c", b=128)[t]))

                @BV
                def _(e, t=t):
                    W(e, toks[t])
                    vm.append(sV.inc(e.reduce_sum(out=st(SUM, t, 1),
                                                  in_=lnin[:, (t % 2) * E:(t % 2) * E + E],
                                                  axis=AXX)))

                @BS
                def _(e, t=t):
                    W(e, toks[t])
                    sm.append(sqs_square(e, lnin[:, (t % 2) * E:(t % 2) * E + E],
                                         st(SQ, t, 1)))

            @BV
            def _(e):
                e.wait_ge(sS.h, sm[-1])
                e.tensor_scalar(out=st(MU), in0=st(SUM), scalar1=1.0 / E, scalar2=None,
                                op0=ALU.mult)
                e.tensor_scalar(out=st(VAR), in0=st(SQ), scalar1=1.0 / E, scalar2=None,
                                op0=ALU.mult)
                e.tensor_tensor(out=st(RS), in0=st(MU), in1=st(MU), op=ALU.mult)
                sV.inc(e.tensor_tensor(out=st(VAR), in0=st(VAR), in1=st(RS), op=ALU.subtract))

            @BS
            def _(e):
                e.wait_ge(sV.h, sV.v)
                sS.inc(e.activation(out=st(RS), in_=st(VAR), func=AF.Sqrt, bias=epsln[:, 0:1]))

            @BV
            def _(e):
                e.wait_ge(sS.h, sS.v)
                if do_rms:
                    e.reciprocal(out=st(RS), in_=st(RS))
                    e.tensor_tensor(out=st(RSR), in0=st(VAR), in1=st(RS), op=ALU.mult)
                    sV.inc(e.tensor_tensor(out=st(RSR), in0=st(RSR), in1=st(RS), op=ALU.mult))
                else:
                    sV.inc(e.reciprocal(out=st(RS), in_=st(RS)))

            if do_rms:
                @BS
                def _(e):
                    e.wait_ge(sV.h, sV.v)
                    sS.inc(e.activation(out=st(RSR), in_=st(RSR), func=AF.Sqrt,
                                        bias=epsrm[:, 0:1]))

                @BV
                def _(e):
                    e.wait_ge(sS.h, sS.v)
                    sV.inc(e.reciprocal(out=st(RSR), in_=st(RSR)))

            am = []
            toks2 = []
            for t in range(NT):
                @block.sync
                def _(e, t=t):
                    if t < 2:
                        e.wait_ge(sV.h, vm[-1])
                        e.wait_ge(sS.h, sm[-1])
                    else:
                        e.wait_ge(sV.h, am[t - 2])
                    toks2.append(rLn.dma(e, t % 2,
                                         lnin[:, (t % 2) * E:(t % 2) * E + E],
                                         src_dram.rearrange("(a b) c -> a b c", b=128)[t]))

                @BV
                def _(e, t=t):
                    W(e, toks2[t])
                    am.append(sV.inc(e.tensor_scalar(
                        out=out_sb[:, t * E:(t + 1) * E],
                        in0=lnin[:, (t % 2) * E:(t % 2) * E + E],
                        scalar1=st(MU, t, 1), scalar2=st(RS, t, 1),
                        op0=ALU.subtract, op1=ALU.mult)))
            lnctx.close()

        def emit_t2f(src_ap_fn, evac_out_ap):
            @block.tensor
            def _(e):
                e.wait_ge(sV.h, sV.v)
                for fc in range(KC):
                    mm = e.transpose(out=pb[:, fc * 128:(fc + 1) * 128],
                                     in_=src_ap_fn(fc), identity=idbf[:, :])
                sT.inc(mm)

            @BV
            def _(e):
                e.wait_ge(sT.h, sT.v)
                sV.inc(e.tensor_copy(out=evac_out_ap,
                                     in_=pb[:, :].rearrange("p (a c) -> p a c", a=KC)))

        emit_ln(ap["X"], xn, do_rms=(upto == "st1"))
        barrier()
        if upto == "st1":
            @BS
            def _(e):
                sO.inc(e.dma_start(out=Y[0:128, 0:128], in_=stat[:, :]), 16)
                e.wait_ge(sO.h, sO.v)
            return nc

        def dump_bf16_to_Y(src_sb):
            dbg = EC(nc.sbuf_tensor("dbgf32", [128, 2 * E], F32))
            dmk = []
            for t in range(NT):
                @BV
                def _(e, t=t):
                    if t >= 2:
                        e.wait_ge(sO.h, dmk[t - 2])
                    sV.inc(e.tensor_copy(out=dbg[:, (t % 2) * E:(t % 2) * E + E],
                                         in_=src_sb[:, t * E:(t + 1) * E]))

                @BS
                def _(e, t=t):
                    e.wait_ge(sV.h, sV.v)
                    sO.inc(e.dma_start(out=Y.rearrange("(a b) c -> a b c", b=128)[t],
                                       in_=dbg[:, (t % 2) * E:(t % 2) * E + E]), 16)
                    e.wait_ge(sO.h, sO.v)
                    dmk.append(sO.v)

        def dump_dram_to_Y(src_dram):
            dbg = EC(nc.sbuf_tensor("dbgf32b", [128, 2 * E], F32))
            dmk = []
            for t in range(NT):
                tok = [None]

                @block.sync
                def _(e, t=t):
                    if t >= 2:
                        e.wait_ge(sO.h, dmk[t - 2])
                    tok[0] = rXI.dma(e, t % 2, dbg[:, (t % 2) * E:(t % 2) * E + E],
                                     src_dram.rearrange("(a b) c -> a b c", b=128)[t])

                @BS
                def _(e, t=t):
                    W(e, tok[0])
                    sO.inc(e.dma_start(out=Y.rearrange("(a b) c -> a b c", b=128)[t],
                                       in_=dbg[:, (t % 2) * E:(t % 2) * E + E]), 16)
                    e.wait_ge(sO.h, sO.v)
                    dmk.append(sO.v)

        if upto == "ln1":
            dump_bf16_to_Y(xn)
            return nc

        def dbg_end():
            @BS
            def _(e):
                sO.inc(e.dma_start(out=Y[0:128, 0:128], in_=stat[:, :]), 16)
                e.wait_ge(sO.h, sO.v)


        escope(_sc[0])
        _sc[0] = scope("P2")
        with ExitStack() as p2:
            E2 = p2.enter_context
            wbuf = E2(nc.sbuf_tensor("wbuf", [128, 2 * E], BF16))
            wg1s = E2(nc.sbuf_tensor("wg1s", [128, KC * D], BF16))
            wg2s = E2(nc.sbuf_tensor("wg2s", [64, E], BF16))
            xnF = E2(nc.sbuf_tensor("xnF", [128, KC * SLAB], BF16))
            qF = E2(nc.sbuf_tensor("qF", [128, KC * SLAB], BF16))
            ktok = E2(nc.sbuf_tensor("ktok", [128, 4 * E], BF16))
            vtok = E2(nc.sbuf_tensor("vtok", [128, 4 * E], BF16))
            ogF = E2(nc.sbuf_tensor("ogF", [128, KC * SLAB], BF16))
            kFb = E2(nc.sbuf_tensor("kFb", [128, KC * CH], BF16))
            qdec = E2(nc.sbuf_tensor("qdec", [128, KC * CH], BF16))
            kdec = E2(nc.sbuf_tensor("kdec", [128, E], BF16))
            msb = E2(nc.sbuf_tensor("msb", [128, 2 * CH], BF16))
            g1F = E2(nc.sbuf_tensor("g1F", [64, SLAB], BF16))
            gate = E2(nc.sbuf_tensor("gate", [128, E], BF16))
            tA = E2(nc.sbuf_tensor("tA", [128, E], BF16))
            tB = E2(nc.sbuf_tensor("tB", [128, E], BF16))
            kvs = E2(nc.sbuf_tensor("kvs", [128, (H // 2) * D], F32))
            kvb = E2(nc.sbuf_tensor("kvb", [128, (H // 2) * D], BF16))
            x0t = E2(nc.sbuf_tensor("x0t", [128, 4 * E], F32))
            x1t = E2(nc.sbuf_tensor("x1t", [128, 4 * E], F32))

            @BV
            def _(e):
                e.memset(kvs[:, :], 0.0)
                sV.inc(e.memset(kvb[:, :], 0.0))

            @block.gpsimd
            def _(e):
                sWg.inc(e.dma_start(out=wg1s[:, :],
                                    in_=ap["Wg1"].rearrange("(a b) c -> b a c", b=128)), 16)
                sWg.inc(e.dma_start(out=wg2s[:, :], in_=ap["Wg2"]), 16)

            wb_guard = [0]

            def wsweep(wname, body):
                kmarks = []
                for k in range(KC):
                    tk = [None]

                    @block.gpsimd
                    def _(e, k=k):
                        if k >= 2:
                            e.wait_ge(sT.h, kmarks[k - 2])
                        elif wb_guard[0]:
                            e.wait_ge(sT.h, wb_guard[0])
                        tk[0] = rWb.dma(e, k % 2,
                                        wbuf[:, (k % 2) * E:(k % 2) * E + E],
                                        ap[wname][k * 128:(k + 1) * 128, :])

                    @block.tensor
                    def _(e, k=k):
                        W(e, tk[0])
                        body(e, k, wbuf[:, (k % 2) * E:(k % 2) * E + E])
                        kmarks.append(sT.v)
                wb_guard[0] = kmarks[-1]

            for sl in range(4):
                c0 = sl * 4
                tok0 = sl * SLAB

                for cc in range(4):
                    tl = c0 + cc
                    emit_t2f(lambda fc, tl=tl: xn[:, tl * E + fc * 128: tl * E + (fc + 1) * 128],
                             xnF[:, :].rearrange("p (a b c) -> p a b c", a=KC, b=4)[:, :, cc, :])

                qbanks = {}

                def qbody_factory(mg):
                    def qbody(e, k, wk):
                        if k == 0:
                            e.wait_ge(sV.h, sV.v)
                            for mi in range(4):
                                qbanks[(mg, mi)] = pool.acquire(e, 1)
                        for mi in range(4):
                            m = mg * 4 + mi
                            b0, pp = qbanks[(mg, mi)]
                            sT.inc(e.matmul(out=pp[:, 0:SLAB],
                                            lhsT=wk[:, m * 128:(m + 1) * 128],
                                            rhs=fchunk(xnF, SLAB)[:, k, :],
                                            start=(k == 0), stop=(k == KC - 1)))
                    return qbody

                for mg in range(2):
                    wsweep("Wq", qbody_factory(mg))

                    @BS
                    def _(e, mg=mg):
                        e.wait_ge(sT.h, sT.v)
                        for mi in range(4):
                            m = mg * 4 + mi
                            b0, pp = qbanks[(mg, mi)]
                            v = sS.inc(e.activation(out=qF[:, m * SLAB:(m + 1) * SLAB],
                                                    in_=pp[:, 0:SLAB], func=AF.Silu))
                            pool.release(b0, 1, sS.h, v)

                for wname, dst in (("Wk", ktok), ("Wv", vtok)):
                    for wave in range(2):
                        kvbanks = {}

                        def kvbody(e, k, wk, wave=wave):
                            if k == 0:
                                for tci in range(2):
                                    kvbanks[tci] = pool.acquire(e, 2)
                            for tci in range(2):
                                tc = wave * 2 + tci
                                b0, pp = kvbanks[tci]
                                for half in range(2):
                                    sT.inc(e.matmul(
                                        out=pp[:, half * 512:(half + 1) * 512],
                                        lhsT=fchunk(xnF, SLAB)[:, k, tc * 128:(tc + 1) * 128],
                                        rhs=wk[:, half * 512:(half + 1) * 512],
                                        start=(k == 0), stop=(k == KC - 1)))

                        wsweep(wname, kvbody)

                        @BS
                        def _(e, wave=wave, dst=dst):
                            e.wait_ge(sT.h, sT.v)
                            for tci in range(2):
                                tc = wave * 2 + tci
                                b0, pp = kvbanks[tci]
                                v = sS.inc(e.activation(out=dst[:, tc * E:(tc + 1) * E],
                                                        in_=pp[:, 0:E], func=AF.Silu))
                                pool.release(b0, 2, sS.h, v)

                g1bank = []

                @block.tensor
                def _(e):
                    b0, pp = pool.acquire(e, 1)
                    g1bank.append((b0, pp))
                    e.wait_ge(sWg.h, 32)
                    for k in range(KC):
                        sT.inc(e.matmul(out=pp[0:64, 0:SLAB],
                                        lhsT=wg1s[:, k * D:(k + 1) * D],
                                        rhs=fchunk(xnF, SLAB)[:, k, :],
                                        start=(k == 0), stop=(k == KC - 1)))

                @BS
                def _(e):
                    b0, pp = g1bank[0]
                    e.wait_ge(sT.h, sT.v)
                    v = sS.inc(e.activation(out=g1F[:, :], in_=pp[0:64, 0:SLAB],
                                            func=AF.Identity))
                    pool.release(b0, 1, sS.h, v)

                if upto == "p2s0q":
                    break

                @block.sync
                def _(e, c0=c0, sl=sl):
                    if sl:
                        e.wait_ge(sO.h, sO.v)
                    for cc in range(4):
                        sX0.inc(e.dma_start(out=x0t[:, cc * E:(cc + 1) * E],
                                            in_=ap["X"].rearrange("(a b) c -> a b c", b=128)[c0 + cc]), 16)
                x0tok = sX0.tok()

                for cc in range(4):
                    emit_t2f(lambda fc, cc=cc: ktok[:, cc * E + fc * 128: cc * E + (fc + 1) * 128],
                             kFb[:, :].rearrange("p (a c) -> p a c", a=KC))

                    @BV
                    def _(e, cc=cc):
                        sV.inc(e.tensor_tensor(out=kdec[:, :], in0=ktok[:, cc * E:(cc + 1) * E],
                                               in1=lamkT[:, :], op=ALU.mult))
                        for fc in range(KC):
                            sV.inc(e.tensor_tensor(
                                out=qdec[:, fc * CH:(fc + 1) * CH],
                                in0=qF[:, fc * SLAB + cc * CH: fc * SLAB + (cc + 1) * CH],
                                in1=lamqF[:, fc * CH:(fc + 1) * CH], op=ALU.mult))

                    gbank = []

                    @block.tensor
                    def _(e, cc=cc):
                        b0, pp = pool.acquire(e, 2)
                        gbank.append((b0, pp))
                        e.wait_ge(sS.h, sS.v)
                        for half in range(2):
                            sT.inc(e.matmul(out=pp[:, half * 512:(half + 1) * 512],
                                            lhsT=g1F[:, cc * CH:(cc + 1) * CH],
                                            rhs=wg2s[:, half * 512:(half + 1) * 512],
                                            start=True, stop=True))

                    @BS
                    def _(e):
                        b0, pp = gbank[0]
                        e.wait_ge(sT.h, sT.v)
                        v = sS.inc(e.activation(out=gate[:, :], in_=pp[:, 0:E],
                                                func=AF.Sigmoid))
                        pool.release(b0, 2, sS.h, v)

                    obank = []

                    @block.tensor
                    def _(e):
                        obank.append(pool.acquire(e, 2))
                        obank.append(pool.acquire(e, 2))

                    mmarks = []
                    kvm = []
                    for h in range(H):
                        fc, ro = h // 2, (h % 2) * D

                        @block.tensor
                        def _(e, h=h, fc=fc, ro=ro, cc=cc):
                            b1, p1 = obank[1]
                            if h == 0:
                                e.wait_ge(sV.h, sV.v)
                            else:
                                e.wait_ge(sV.h, mmarks[h - 1])
                            sT.inc(e.matmul(out=p1[:, 0:CH],
                                            lhsT=kFb[ro:ro + D, fc * CH:(fc + 1) * CH],
                                            rhs=qF[ro:ro + D, fc * SLAB + cc * CH: fc * SLAB + (cc + 1) * CH],
                                            start=True, stop=True))

                        scm = sT.v

                        @BV
                        def _(e, h=h, scm=scm):
                            b1, p1 = obank[1]
                            e.wait_ge(sT.h, scm)
                            mmarks.append(sV.inc(e.scalar_tensor_tensor(
                                out=msb[:, (h % 2) * CH:(h % 2) * CH + CH],
                                in0=p1[:, 0:CH], scalar=1.0,
                                in1=maskT[:, h * CH:(h + 1) * CH],
                                op0=ALU.mult, op1=ALU.mult)))

                        @block.tensor
                        def _(e, h=h, fc=fc, ro=ro, cc=cc):
                            b0, pp = obank[0]
                            b1, p1 = obank[1]
                            e.wait_ge(sV.h, mmarks[h])
                            if h > 0:
                                e.wait_ge(sV.h, kvm[h - 1])
                            sT.inc(e.matmul(out=pp[:, h * D:(h + 1) * D],
                                            lhsT=msb[:, (h % 2) * CH:(h % 2) * CH + CH],
                                            rhs=vtok[:, cc * E + h * D: cc * E + (h + 1) * D],
                                            start=(h == 0 or h == 8), stop=False))
                            sT.inc(e.matmul(out=pp[:, h * D:(h + 1) * D],
                                            lhsT=qdec[ro:ro + D, fc * CH:(fc + 1) * CH],
                                            rhs=kvb[ro:ro + D, (h // 2) * D:(h // 2 + 1) * D],
                                            start=False, stop=(h == 7 or h == 15)))
                            sT.inc(e.matmul(out=p1[ro:ro + D, 512:512 + D],
                                            lhsT=kdec[:, h * D:(h + 1) * D],
                                            rhs=vtok[:, cc * E + h * D: cc * E + (h + 1) * D],
                                            start=True, stop=True,
                                            tile_position=(0, ro)))

                        kvt = sT.v

                        @BV
                        def _(e, h=h, kvt=kvt):
                            b1, p1 = obank[1]
                            e.wait_ge(sT.h, kvt)
                            ro2 = (h % 2) * D
                            e.scalar_tensor_tensor(
                                out=kvs[ro2:ro2 + D, (h // 2) * D:(h // 2 + 1) * D],
                                in0=kvs[ro2:ro2 + D, (h // 2) * D:(h // 2 + 1) * D],
                                scalar=float(lam_c[h]),
                                in1=p1[ro2:ro2 + D, 512:512 + D],
                                op0=ALU.mult, op1=ALU.add)
                            kvm.append(sV.inc(e.tensor_copy(
                                out=kvb[ro2:ro2 + D, (h // 2) * D:(h // 2 + 1) * D],
                                in_=kvs[ro2:ro2 + D, (h // 2) * D:(h // 2 + 1) * D])))

                    if upto == "p2s0c0h" and cc == 0:
                        break

                    redm = [0]
                    if upto != "p2e0b":
                        @BV
                        def _(e):
                            b0, pp = obank[0]
                            e.wait_ge(sT.h, sT.v)
                            redm[0] = sV.inc(e.reduce_sum(out=t1a[:, :],
                                                          in_=pp[:, 0:E], axis=AXX))

                    if upto == "p2e0a" and cc == 0:
                        break

                    @BS
                    def _(e):
                        b0, pp = obank[0]
                        e.wait_ge(sT.h, sT.v)
                        if redm[0]:
                            e.wait_ge(sV.h, redm[0])
                        sqs_square(e, pp[:, 0:E], t1b[:, :])

                    if upto == "p2e0b" and cc == 0:
                        break

                    if upto == "p2e1" and cc == 0:
                        break

                    @BV
                    def _(e):
                        e.wait_ge(sS.h, sS.v)
                        e.tensor_scalar(out=t1a[:, :], in0=t1a[:, :], scalar1=1.0 / E,
                                        scalar2=None, op0=ALU.mult)
                        e.tensor_scalar(out=t1b[:, :], in0=t1b[:, :], scalar1=1.0 / E,
                                        scalar2=None, op0=ALU.mult)
                        e.tensor_scalar(out=t8a[:, 0:1], in0=t1a[:, :], scalar1=t1a[:, 0:1],
                                        scalar2=None, op0=ALU.mult)
                        sV.inc(e.tensor_tensor(out=t1b[:, :], in0=t1b[:, :],
                                               in1=t8a[:, 0:1], op=ALU.subtract))

                    if upto == "p2e2" and cc == 0:
                        break

                    @BS
                    def _(e):
                        e.wait_ge(sV.h, sV.v)
                        sS.inc(e.activation(out=t1b[:, :], in_=t1b[:, :], func=AF.Sqrt,
                                            bias=epsln[:, 0:1]))

                    if upto == "p2e3" and cc == 0:
                        break

                    @BV
                    def _(e):
                        b0, pp = obank[0]
                        e.wait_ge(sS.h, sS.v)
                        e.reciprocal(out=t1b[:, :], in_=t1b[:, :])
                        e.tensor_scalar(out=tA[:, :], in0=pp[:, 0:E],
                                        scalar1=t1a[:, 0:1], scalar2=t1b[:, 0:1],
                                        op0=ALU.subtract, op1=ALU.mult)
                        e.tensor_tensor(out=tB[:, :], in0=tA[:, :], in1=gammaB[:, :], op=ALU.mult)
                        e.tensor_tensor(out=tA[:, :], in0=tB[:, :], in1=betaB[:, :], op=ALU.add)
                        v = sV.inc(e.tensor_tensor(out=tB[:, :], in0=tA[:, :],
                                                   in1=gate[:, :], op=ALU.mult))
                        pool.release(obank[0][0], 2, sV.h, v)
                        pool.release(obank[1][0], 2, sV.h, v)

                    if upto == "p2e4" and cc == 0:
                        break

                    emit_t2f(lambda fc: tB[:, fc * 128:(fc + 1) * 128],
                             ogF[:, :].rearrange("p (a b c) -> p a b c", a=KC, b=4)[:, :, cc, :])

                    if upto == "p2s0c0e" and cc == 0:
                        break

                if upto in ("p2s0c0h", "p2s0c0e", "p2e0a", "p2e0b",
                            "p2e1", "p2e2", "p2e3", "p2e4"):
                    break

                for wave in range(2):
                    xbanks = {}

                    def obody(e, k, wk, wave=wave):
                        if k == 0:
                            e.wait_ge(sV.h, sV.v)
                            for tci in range(2):
                                xbanks[tci] = pool.acquire(e, 2)
                        for tci in range(2):
                            tc = wave * 2 + tci
                            b0, pp = xbanks[tci]
                            for half in range(2):
                                sT.inc(e.matmul(
                                    out=pp[:, half * 512:(half + 1) * 512],
                                    lhsT=fchunk(ogF, SLAB)[:, k, tc * 128:(tc + 1) * 128],
                                    rhs=wk[:, half * 512:(half + 1) * 512],
                                    start=(k == 0), stop=(k == KC - 1)))

                    wsweep("Wo", obody)

                    @BV
                    def _(e, wave=wave, x0tok=x0tok):
                        e.wait_ge(sT.h, sT.v)
                        W(e, x0tok)
                        for tci in range(2):
                            tc = wave * 2 + tci
                            b0, pp = xbanks[tci]
                            v = sV.inc(e.tensor_tensor(out=x1t[:, tc * E:(tc + 1) * E],
                                                       in0=x0t[:, tc * E:(tc + 1) * E],
                                                       in1=pp[:, 0:E], op=ALU.add))
                            pool.release(b0, 2, sV.h, v)

                    @BS
                    def _(e, wave=wave, c0=c0):
                        e.wait_ge(sV.h, sV.v)
                        for tci in range(2):
                            tc = wave * 2 + tci
                            sO.inc(e.dma_start(
                                out=X1.rearrange("(a b) c -> a b c", b=128)[c0 + tc],
                                in_=x1t[:, tc * E:(tc + 1) * E]), 16)
                        e.wait_ge(sO.h, sO.v)

                if upto == "p2s0":
                    break

        barrier()
        if upto in ("p2s0q", "p2s0", "p2s0c0h", "p2s0c0e", "p2e0a", "p2e0b",
                    "p2e1", "p2e2", "p2e3", "p2e4"):
            dbg_end()
            return nc
        if upto == "p2n":
            dbg_end()
            return nc
        if upto == "p2":
            dump_dram_to_Y(X1)
            return nc

        escope(_sc[0])
        _sc[0] = scope("P3a")
        emit_ln(X1, xn, do_rms=True)

        @block.gpsimd
        def _(e):
            e.wait_ge(sV.h, sV.v)
            sG.inc(e.dma_start(out=xf_in.ap().rearrange("(a b) c -> b a c", b=128),
                               in_=xn[:, :].rearrange("p (a c) -> p a c", a=NT)), 16)
            e.wait_ge(sG.h, sG.v)
            sC.inc(e.collective_compute(
                "AllGather", ALU.bypass, replica_groups=[list(range(NCORE))],
                ins=[xf_in.ap().opt()], outs=[xf_all.ap().opt()]))

        xfF = EC(nc.sbuf_tensor("xfF", [128, KC * T], BF16))
        for tl in range(NT):
            emit_t2f(lambda fc, tl=tl: xn[:, tl * E + fc * 128: tl * E + (fc + 1) * 128],
                     xfF[:, :].rearrange("p (a b c) -> p a b c", a=KC, b=NT)[:, :, tl, :])

        escope(_sc[0])
        _sc[0] = scope("P3b")
        lgF = EC(nc.sbuf_tensor("lgF", [8, T], F32))
        rb = []

        @block.tensor
        def _(e):
            b0, pp = pool.acquire(e, 4)
            rb.append((b0, pp))
            e.wait_ge(sV.h, sV.v)
            for k in range(KC):
                for q in range(4):
                    sT.inc(e.matmul(out=pp[0:8, q * 512:(q + 1) * 512],
                                    lhsT=rwTs[:, k * 8:(k + 1) * 8],
                                    rhs=fchunk(xfF, T)[:, k, q * 512:(q + 1) * 512],
                                    start=(k == 0), stop=(k == KC - 1)))

        @BV
        def _(e):
            b0, pp = rb[0]
            e.wait_ge(sT.h, sT.v)
            v = sV.inc(e.tensor_copy(out=lgF[:, :], in_=pp[0:8, 0:T]))
            pool.release(b0, 4, sV.h, v)

        rb2 = []

        @block.tensor
        def _(e):
            b0, pp = pool.acquire(e, 1)
            rb2.append((b0, pp))
            e.wait_ge(sV.h, sV.v)
            for tl in range(NT):
                mm = e.transpose(out=pp[:, tl * 8:(tl + 1) * 8],
                                 in_=lgF[:, tl * 128:(tl + 1) * 128],
                                 identity=id32[0:8, 0:8])
            sT.inc(mm)

        @BV
        def _(e):
            b0, pp = rb2[0]
            e.wait_ge(sT.h, sT.v)
            v = None
            for tl in range(NT):
                v = sV.inc(e.tensor_scalar(out=lg[:, tl * 8:(tl + 1) * 8],
                                           in0=pp[:, tl * 8:(tl + 1) * 8],
                                           scalar1=st(RSR, tl, 1), scalar2=None,
                                           op0=ALU.mult))
            pool.release(b0, 1, sV.h, v)

        @BV
        def _(e):
            for tl in range(NT):
                lgt = lg[:, tl * 8:(tl + 1) * 8]
                m1 = m8a[:, tl * 8: tl * 8 + 1]
                m2 = m8a[:, tl * 8 + 1: tl * 8 + 2]
                e.max(out=m8a[:, tl * 8:(tl + 1) * 8], in_=lgt)
                e.tensor_scalar(out=t8a[:, :], in0=lgt, scalar1=m1, scalar2=None,
                                op0=ALU.is_equal)
                e.tensor_tensor(out=t8a[:, :], in0=t8a[:, :], in1=bmi[:, :], op=ALU.mult)
                e.reduce_max(out=t1a[:, :], in_=t8a[:, :], axis=AXX)
                e.tensor_scalar(out=i1a[:, tl:tl + 1], in0=t1a[:, :], scalar1=-1.0,
                                scalar2=BIG, op0=ALU.mult, op1=ALU.add)
                e.tensor_scalar(out=t8a[:, :], in0=iota8[:, :], scalar1=i1a[:, tl:tl + 1],
                                scalar2=None, op0=ALU.is_equal)
                e.tensor_copy(out=Ma[:, tl * 8:(tl + 1) * 8], in_=t8a[:, :])
                e.tensor_scalar(out=t8b[:, :], in0=lgt, scalar1=m2, scalar2=None,
                                op0=ALU.is_equal)
                e.tensor_tensor(out=t8b[:, :], in0=t8b[:, :], in1=bmi[:, :], op=ALU.mult)
                e.tensor_scalar(out=t8a[:, :], in0=t8a[:, :], scalar1=-BIG, scalar2=None,
                                op0=ALU.mult)
                e.tensor_tensor(out=t8b[:, :], in0=t8b[:, :], in1=t8a[:, :], op=ALU.add)
                e.reduce_max(out=t1a[:, :], in_=t8b[:, :], axis=AXX)
                e.tensor_scalar(out=i2a[:, tl:tl + 1], in0=t1a[:, :], scalar1=-1.0,
                                scalar2=BIG, op0=ALU.mult, op1=ALU.add)
                e.tensor_scalar(out=t8b[:, :], in0=iota8[:, :], scalar1=i2a[:, tl:tl + 1],
                                scalar2=None, op0=ALU.is_equal)
                e.tensor_tensor(out=Ma[:, tl * 8:(tl + 1) * 8],
                                in0=Ma[:, tl * 8:(tl + 1) * 8], in1=t8b[:, :], op=ALU.add)
                e.tensor_tensor(out=st(W1, tl, 1), in0=m1, in1=m2, op=ALU.subtract)
            sV.inc(e.tensor_copy(out=t1b[:, :], in_=t1a[:, :]))

        @BS
        def _(e):
            e.wait_ge(sV.h, sV.v)
            sS.inc(e.activation(out=st(W1), in_=st(W1), func=AF.Sigmoid))

        @BV
        def _(e):
            e.wait_ge(sS.h, sS.v)
            sV.inc(e.tensor_scalar(out=st(W2), in0=st(W1), scalar1=-1.0, scalar2=1.0,
                                   op0=ALU.mult, op1=ALU.add))

        rankb = []

        @block.tensor
        def _(e):
            rankb.append(pool.acquire(e, 1))
            rankb.append(pool.acquire(e, 1))

        @BV
        def _(e):
            sV.inc(e.memset(Rrow[:, :], 0.0))

        for tl in range(NT):
            @block.tensor
            def _(e, tl=tl):
                b0, pp = rankb[0]
                b1, p1 = rankb[1]
                e.wait_ge(sV.h, sV.v)
                sT.inc(e.matmul(out=pp[:, 0:8], lhsT=pfx[:, :],
                                rhs=Ma[:, tl * 8:(tl + 1) * 8], start=True, stop=False))
                sT.inc(e.matmul(out=pp[:, 0:8], lhsT=onesr[:, :], rhs=Rrow[:, :],
                                start=False, stop=True))
                sT.inc(e.matmul(out=p1[0:1, 0:8], lhsT=onesc[:, :],
                                rhs=Ma[:, tl * 8:(tl + 1) * 8],
                                start=True, stop=True))

            @BV
            def _(e, tl=tl):
                b0, pp = rankb[0]
                b1, p1 = rankb[1]
                e.wait_ge(sT.h, sT.v)
                e.tensor_copy(out=ranka[:, tl * 8:(tl + 1) * 8], in_=pp[:, 0:8])
                sV.inc(e.tensor_tensor(out=Rrow[:, :], in0=Rrow[:, :],
                                       in1=p1[0:1, 0:8], op=ALU.add))

        @block.gpsimd
        def _(e):
            e.wait_ge(sV.h, sV.v)
            sG.inc(e.dma_start(out=cnt_in[:, :], in_=Rrow[:, :]), 16)
            e.wait_ge(sG.h, sG.v)
            sC.inc(e.collective_compute(
                "AllGather", ALU.bypass, replica_groups=[list(range(NCORE))],
                ins=[cnt_in.ap().opt()], outs=[cnt_all.ap().opt()]))
            e.wait_ge(sC.h, sC.v)
            sG.inc(e.dma_start(out=cnt8[:, :], in_=cnt_all[:, :]), 16)
            e.wait_ge(sG.h, sG.v)

        @block.tensor
        def _(e):
            b0, pp = rankb[0]
            e.wait_ge(sG.h, sG.v)
            e.wait_ge(sV.h, sV.v)
            sT.inc(e.matmul(out=pp[:, 0:8], lhsT=prefm[:, :], rhs=cnt8[:, :],
                            start=True, stop=True))

        @BV
        def _(e):
            b0, pp = rankb[0]
            e.wait_ge(sT.h, sT.v)
            v = sV.inc(e.tensor_copy(out=Osb[:, :], in_=pp[:, 0:8]))
            pool.release(rankb[0][0], 1, sV.h, v)
            pool.release(rankb[1][0], 1, sV.h, v)

        @BV
        def _(e):
            for tl in range(NT):
                for which, ia in ((0, i1a), (1, i2a)):
                    e.tensor_scalar(out=t8a[:, :], in0=iota8[:, :],
                                    scalar1=ia[:, tl:tl + 1], scalar2=None, op0=ALU.is_equal)
                    e.tensor_tensor(out=t8b[:, :], in0=ranka[:, tl * 8:(tl + 1) * 8],
                                    in1=t8a[:, :], op=ALU.mult)
                    e.reduce_sum(out=t1a[:, :], in_=t8b[:, :], axis=AXX)
                    e.tensor_tensor(out=t8b[:, :], in0=Osb[:, :], in1=t8a[:, :], op=ALU.mult)
                    e.reduce_sum(out=t1b[:, :], in_=t8b[:, :], axis=AXX)
                    e.tensor_tensor(out=t1a[:, :], in0=t1a[:, :], in1=t1b[:, :], op=ALU.add)
                    e.tensor_copy(out=meta[:, tl * 4 + 2 * which: tl * 4 + 2 * which + 1],
                                  in_=ia[:, tl:tl + 1])
                    e.tensor_copy(out=meta[:, tl * 4 + 2 * which + 1: tl * 4 + 2 * which + 2],
                                  in_=t1a[:, :])
                    e.tensor_scalar(out=t1a[:, :], in0=t1a[:, :], scalar1=float(CAP),
                                    scalar2=None, op0=ALU.min)
                    e.tensor_scalar(out=t1b[:, :], in0=ia[:, tl:tl + 1],
                                    scalar1=float(CAP1), scalar2=None, op0=ALU.mult)
                    e.tensor_tensor(out=t1a[:, :], in0=t1a[:, :], in1=t1b[:, :], op=ALU.add)
                    e.tensor_copy(out=loca[:, which * 16 + tl: which * 16 + tl + 1],
                                  in_=t1a[:, :])
            sV.inc(e.tensor_copy(out=t1b[:, :], in_=t1a[:, :]))

        @block.gpsimd
        def _(e):
            e.wait_ge(sV.h, sV.v)
            sG.inc(e.dma_start(out=meta_in.ap().rearrange("(a b) c -> b a c", b=128),
                               in_=meta[:, :].rearrange("p (a c) -> p a c", a=NT)), 16)
            e.wait_ge(sG.h, sG.v)
            sC.inc(e.collective_compute(
                "AllGather", ALU.bypass, replica_groups=[list(range(NCORE))],
                ins=[meta_in.ap().opt()], outs=[meta_all.ap().opt()]))

        escope(_sc[0])
        _sc[0] = scope("P3c")
        mslab = EC(nc.sbuf_tensor("mslab", [128, 512], F32))
        slotv = EC(nc.sbuf_tensor("slotv", [128, 128], F32))
        sloti = EC(nc.sbuf_tensor("sloti", [128, 128], I32))
        tmA = EC(nc.sbuf_tensor("tmA", [128, 128], F32))
        tmB = EC(nc.sbuf_tensor("tmB", [128, 128], F32))
        tmC = EC(nc.sbuf_tensor("tmC", [128, 128], F32))

        @block.gpsimd
        def _(e):
            e.wait_ge(sC.h, sC.v)
            sG.inc(e.dma_start(out=mslab[:, :],
                               in_=meta_all.ap().rearrange("(a b) c -> a (b c)", a=128)), 16)
            e.wait_ge(sG.h, sG.v)

        @BV
        def _(e):
            e.wait_ge(sG.h, sG.v)
            mv = mslab[:, :].rearrange("p (c f) -> p f c", f=4)
            i1v, g1v, i2v, g2v = mv[:, 0, :], mv[:, 1, :], mv[:, 2, :], mv[:, 3, :]
            e.tensor_scalar(out=tmA[:, :], in0=i1v, scalar1=mye[:, 0:1], scalar2=None,
                            op0=ALU.is_equal)
            e.tensor_scalar(out=tmB[:, :], in0=g1v, scalar1=float(CAP), scalar2=None,
                            op0=ALU.is_lt)
            e.tensor_tensor(out=tmA[:, :], in0=tmA[:, :], in1=tmB[:, :], op=ALU.mult)
            e.tensor_tensor(out=slotv[:, :], in0=tmA[:, :], in1=g1v, op=ALU.mult)
            e.tensor_copy(out=tmC[:, :], in_=tmA[:, :])
            e.tensor_scalar(out=tmA[:, :], in0=i2v, scalar1=mye[:, 0:1], scalar2=None,
                            op0=ALU.is_equal)
            e.tensor_scalar(out=tmB[:, :], in0=g2v, scalar1=float(CAP), scalar2=None,
                            op0=ALU.is_lt)
            e.tensor_tensor(out=tmA[:, :], in0=tmA[:, :], in1=tmB[:, :], op=ALU.mult)
            e.tensor_tensor(out=tmB[:, :], in0=tmA[:, :], in1=g2v, op=ALU.mult)
            e.tensor_tensor(out=slotv[:, :], in0=slotv[:, :], in1=tmB[:, :], op=ALU.add)
            e.tensor_tensor(out=tmC[:, :], in0=tmC[:, :], in1=tmA[:, :], op=ALU.add)
            e.tensor_scalar(out=tmC[:, :], in0=tmC[:, :], scalar1=-float(CAP),
                            scalar2=float(CAP), op0=ALU.mult, op1=ALU.add)
            sV.inc(e.tensor_tensor(out=slotv[:, :], in0=slotv[:, :],
                                   in1=tmC[:, :], op=ALU.add))

        escope(_sc[0])
        _sc[0] = scope("P3d")
        p3dstack = ExitStack()
        ohb = p3dstack.enter_context(nc.sbuf_tensor("ohb", [128, 2 * 4 * 640], F32))
        svB = p3dstack.enter_context(nc.sbuf_tensor("svB", [128, 128], F32))
        drow = p3dstack.enter_context(nc.sbuf_tensor("drow", [1, 2560], F32))
        for ps in range(4):
            dpb = []

            @block.tensor
            def _(e):
                dpb.append(pool.acquire(e, 1))
                dpb.append(pool.acquire(e, 1))

            @BV
            def _(e, ps=ps):
                e.wait_ge(sV.h, sV.v)
                sV.inc(e.tensor_scalar(out=svB[:, :], in0=slotv[:, :],
                                       scalar1=1.0, scalar2=-640.0 * ps,
                                       op0=ALU.mult, op1=ALU.add))

            ohm = []
            mtm = []
            for gq in range(32):
                @block.vector
                def _(e, gq=gq):
                    if gq >= 2:
                        e.wait_ge(sT.h, mtm[gq - 2])
                    else:
                        e.wait_ge(sT.h, sT.v)
                        e.wait_ge(sV.h, sV.v)
                    for ci in range(4):
                        c = gq * 4 + ci
                        r = e.tensor_scalar(
                            out=ohb[:, ((gq % 2) * 4 + ci) * 640:((gq % 2) * 4 + ci) * 640 + 640],
                            in0=iota640[:, :], scalar1=svB[:, c:c + 1], scalar2=None,
                            op0=ALU.is_equal)
                    ohm.append(sV.inc(r))

                @block.tensor
                def _(e, gq=gq):
                    e.wait_ge(sV.h, ohm[gq])
                    for ci in range(4):
                        c = gq * 4 + ci
                        off = ((gq % 2) * 4 + ci) * 640
                        sT.inc(e.matmul(out=dpb[0][1][0:1, 0:512],
                                        lhsT=idsP[:, c:c + 1],
                                        rhs=ohb[:, off:off + 512],
                                        start=(c == 0), stop=(c == 127)))
                        sT.inc(e.matmul(out=dpb[1][1][0:1, 0:128],
                                        lhsT=idsP[:, c:c + 1],
                                        rhs=ohb[:, off + 512:off + 640],
                                        start=(c == 0), stop=(c == 127)))
                    mtm.append(sT.v)

            @BV
            def _(e, ps=ps):
                e.wait_ge(sT.h, sT.v)
                e.tensor_copy(out=drow[0:1, ps * 640:ps * 640 + 512],
                              in_=dpb[0][1][0:1, 0:512])
                v = sV.inc(e.tensor_copy(out=drow[0:1, ps * 640 + 512:ps * 640 + 640],
                                         in_=dpb[1][1][0:1, 0:128]))
                pool.release(dpb[0][0], 1, sV.h, v)
                pool.release(dpb[1][0], 1, sV.h, v)

        trm = []
        for j in range(20):
            tb = []

            @block.tensor
            def _(e, j=j):
                tb.append(pool.acquire(e, 1))
                if j == 0:
                    e.wait_ge(sV.h, sV.v)
                sT.inc(e.transpose(out=tb[0][1][0:128, 0:1],
                                   in_=drow[0:1, j * 128:(j + 1) * 128],
                                   identity=id32[0:1, 0:1]))

            @BV
            def _(e, j=j):
                e.wait_ge(sT.h, sT.v)
                v = sV.inc(e.tensor_copy(out=dispi[:, j:j + 1], in_=tb[0][1][0:128, 0:1]))
                pool.release(tb[0][0], 1, sV.h, v)

        p3dstack.close()

        barrier()
        if upto == "p3n":
            dbg_end()
            return nc

        escope(_sc[0])
        _sc[0] = scope("P4")
        with ExitStack() as p4:
            E4 = p4.enter_context
            wtg = E4(nc.sbuf_tensor("wtg", [128, 2 * E], BF16))
            wtu = E4(nc.sbuf_tensor("wtu", [128, 2 * E], BF16))
            wgbuf = E4(nc.sbuf_tensor("wgbuf", [128, 2 * E], BF16))
            wubuf = E4(nc.sbuf_tensor("wubuf", [128, 2 * E], BF16))
            wdbuf = E4(nc.sbuf_tensor("wdbuf", [128, 4 * E], BF16))
            wd67 = E4(nc.sbuf_tensor("wd67", [128, 64 * 256], BF16))
            hsl = E4(nc.sbuf_tensor("hsl", [128, 2 * T], BF16))
            hein = E4(nc.sbuf_tensor("hein4", [128, 4 * 512], BF16))
            gtmp = E4(nc.sbuf_tensor("gtmp", [128, 2048], BF16))
            eoutF = E4(nc.sbuf_tensor("eoutF", [128, 8 * 512], BF16))
            x1sl = E4(nc.sbuf_tensor("x1sl", [128, 4 * E], F32))

            escope(_sc[0])
            _sc[0] = scope("P4g")
            tgm, tum, wcg, wcu, gum, sttm, hstok = [], [], [], [], [], [], []
            for m in range(64):
                ltok = [None]

                @block.gpsimd
                def _(e, m=m):
                    if m >= 2:
                        e.wait_ge(sT.h, tum[m - 2])
                    rWT.dma(e, m % 2, wtg[:, (m % 2) * E:(m % 2) * E + E],
                            ap["sgT"][m * 128:(m + 1) * 128, :])
                    ltok[0] = rWT.dma(e, m % 2, wtu[:, (m % 2) * E:(m % 2) * E + E],
                                      ap["suT"][m * 128:(m + 1) * 128, :])

                @block.tensor
                def _(e, m=m):
                    W(e, ltok[0])
                    if m == 0:
                        e.wait_ge(sV.h, sV.v)
                    else:
                        e.wait_ge(sWC.h, wcg[m - 1])
                    for k in range(KC):
                        mm = e.transpose(out=pb[:, k * 128:(k + 1) * 128],
                                         in_=wtg[:, (m % 2) * E + k * 128:(m % 2) * E + (k + 1) * 128],
                                         identity=idbf[:, :])
                    tgm.append(sT.inc(mm))
                    if m > 0:
                        e.wait_ge(sWC.h, wcu[m - 1])
                    for k in range(KC):
                        mm = e.transpose(out=pb2[:, k * 128:(k + 1) * 128],
                                         in_=wtu[:, (m % 2) * E + k * 128:(m % 2) * E + (k + 1) * 128],
                                         identity=idbf[:, :])
                    tum.append(sT.inc(mm))

                @block.vector
                def _(e, m=m):
                    e.wait_ge(sT.h, tgm[m])
                    if m >= 2:
                        e.wait_ge(sT.h, gum[m - 2])
                    wcg.append(sWC.inc(e.tensor_copy(
                        out=wgbuf[:, (m % 2) * E:(m % 2) * E + E], in_=pb[:, 0:E])))
                    e.wait_ge(sT.h, tum[m])
                    wcu.append(sWC.inc(e.tensor_copy(
                        out=wubuf[:, (m % 2) * E:(m % 2) * E + E], in_=pb2[:, 0:E])))

                gm = []
                um = []
                gacq = []
                uacq = []

                @block.tensor
                def _(e, m=m):
                    gacq.append(pool.acquire(e, 4))
                    e.wait_ge(sWC.h, wcg[m])
                    for k in range(KC):
                        for q in range(4):
                            sT.inc(e.matmul(
                                out=gacq[0][1][:, q * 512:(q + 1) * 512],
                                lhsT=wgbuf[:, (m % 2) * E + k * 128:(m % 2) * E + (k + 1) * 128],
                                rhs=fchunk(xfF, T)[:, k, q * 512:(q + 1) * 512],
                                start=(k == 0), stop=(k == KC - 1)))
                    gm.append(sT.v)

                @BS
                def _(e, m=m):
                    e.wait_ge(sT.h, gm[0])
                    v = sS.inc(e.activation(out=gtmp[:, :], in_=gacq[0][1][:, 0:2048],
                                            func=AF.Silu))
                    pool.release(gacq[0][0], 4, sS.h, v)

                @block.tensor
                def _(e, m=m):
                    uacq.append(pool.acquire(e, 4))
                    e.wait_ge(sWC.h, wcu[m])
                    for k in range(KC):
                        for q in range(4):
                            sT.inc(e.matmul(
                                out=uacq[0][1][:, q * 512:(q + 1) * 512],
                                lhsT=wubuf[:, (m % 2) * E + k * 128:(m % 2) * E + (k + 1) * 128],
                                rhs=fchunk(xfF, T)[:, k, q * 512:(q + 1) * 512],
                                start=(k == 0), stop=(k == KC - 1)))
                    um.append(sT.v)
                    gum.append(sT.v)

                @BV
                def _(e, m=m):
                    e.wait_ge(sT.h, um[0])
                    e.wait_ge(sS.h, sS.v)
                    if m >= 2:
                        W(e, hstok[m - 2])
                    v = sV.inc(e.scalar_tensor_tensor(
                        out=hsl[:, (m % 2) * T:(m % 2) * T + T],
                        in0=uacq[0][1][:, 0:2048], scalar=1.0, in1=gtmp[:, :],
                        op0=ALU.mult, op1=ALU.mult))
                    pool.release(uacq[0][0], 4, sV.h, v)
                    sttm.append(v)

                @block.sync
                def _(e, m=m):
                    e.wait_ge(sV.h, sttm[m])
                    t = None
                    for s in range(4):
                        t = rHS.dma(e, m % 2, HS[s][m * 128:(m + 1) * 128, :],
                                    hsl[:, (m % 2) * T + s * 512:(m % 2) * T + (s + 1) * 512])
                    hstok.append(t)

            wt_guard[0] = tum[-1]

            escope(_sc[0])
            _sc[0] = scope("P4d")
            wd_guard = [0]
            hi4_guard = [0]
            wd67m = []
            for sl in range(4):
                tok0 = sl * SLAB

                @block.sync
                def _(e, sl=sl, tok0=tok0):
                    if sl == 0:
                        for s_ in rHS.s:
                            e.wait_ge(s_.h, s_.v)
                    else:
                        e.wait_ge(sO.h, sO.v)
                    sX1.inc(e.dma_start(
                        out=x1sl[:, :].rearrange("p (a c) -> p a c", a=4),
                        in_=X1[tok0:tok0 + SLAB, :].rearrange("(a b) c -> b a c", b=128)), 16)
                x1tok = sX1.tok()

                dbanks = []

                @block.tensor
                def _(e):
                    for d in range(6):
                        dbanks.append(pool.acquire(e, 1))

                dm = []
                for m in range(64):
                    wtok = [None]
                    htok = [None]

                    @block.gpsimd
                    def _(e, m=m):
                        if m >= 4:
                            e.wait_ge(sT.h, dm[m - 4])
                        elif wd_guard[0]:
                            e.wait_ge(sT.h, wd_guard[0])
                        wtok[0] = rWD.dma(e, m % 4,
                                          wdbuf[:, (m % 4) * E:(m % 4) * E + E],
                                          ap["sd"][m * 128:(m + 1) * 128, :])

                    @block.sync
                    def _(e, m=m, sl=sl):
                        if m >= 4:
                            e.wait_ge(sT.h, dm[m - 4])
                        elif hi4_guard[0]:
                            e.wait_ge(sT.h, hi4_guard[0])
                        htok[0] = rHI.dma(e, m % 4,
                                          hein[:, (m % 4) * 512:(m % 4) * 512 + 512],
                                          HS[sl][m * 128:(m + 1) * 128, :])

                    @block.tensor
                    def _(e, m=m):
                        W(e, wtok[0])
                        W(e, htok[0])
                        if m == 0:
                            e.wait_ge(sV.h, sV.v)
                        for d in range(6):
                            sT.inc(e.matmul(
                                out=dbanks[d][1][:, 0:512],
                                lhsT=wdbuf[:, (m % 4) * E + d * 128:(m % 4) * E + (d + 1) * 128],
                                rhs=hein[:, (m % 4) * 512:(m % 4) * 512 + 512],
                                start=(m == 0), stop=(m == 63)))
                        dm.append(sT.v)

                    if sl == 0:
                        @block.vector
                        def _(e, m=m, wtok=wtok):
                            W(e, wtok[0])
                            wd67m.append(sWC.inc(e.tensor_copy(
                                out=wd67[:, m * 256:(m + 1) * 256],
                                in_=wdbuf[:, (m % 4) * E + 6 * 128:(m % 4) * E + 8 * 128])))

                wd_guard[0] = dm[-1]

                @BV
                def _(e):
                    e.wait_ge(sT.h, sT.v)
                    for d in range(6):
                        b0, pp = dbanks[d]
                        v = sV.inc(e.tensor_copy(out=eoutF[:, d * 512:(d + 1) * 512],
                                                 in_=pp[:, 0:512]))
                        pool.release(b0, 1, sV.h, v)

                d8b = []

                @block.tensor
                def _(e):
                    d8b.append(pool.acquire(e, 1))
                    d8b.append(pool.acquire(e, 1))

                dm2 = []
                for m in range(64):
                    htok = [None]

                    @block.sync
                    def _(e, m=m, sl=sl):
                        if m >= 4:
                            e.wait_ge(sT.h, dm2[m - 4])
                        else:
                            e.wait_ge(sT.h, dm[-1])
                        htok[0] = rHI.dma(e, m % 4,
                                          hein[:, (m % 4) * 512:(m % 4) * 512 + 512],
                                          HS[sl][m * 128:(m + 1) * 128, :])

                    @block.tensor
                    def _(e, m=m, sl=sl):
                        W(e, htok[0])
                        if m == 0 and sl == 0:
                            e.wait_ge(sWC.h, wd67m[-1])
                        for di in range(2):
                            sT.inc(e.matmul(
                                out=d8b[di][1][:, 0:512],
                                lhsT=wd67[:, m * 256 + di * 128:m * 256 + (di + 1) * 128],
                                rhs=hein[:, (m % 4) * 512:(m % 4) * 512 + 512],
                                start=(m == 0), stop=(m == 63)))
                        dm2.append(sT.v)
                hi4_guard[0] = dm2[-1]

                @BV
                def _(e):
                    e.wait_ge(sT.h, sT.v)
                    for di in range(2):
                        b0, pp = d8b[di]
                        v = sV.inc(e.tensor_copy(out=eoutF[:, (6 + di) * 512:(7 + di) * 512],
                                                 in_=pp[:, 0:512]))
                        pool.release(b0, 1, sV.h, v)

                for tg in range(4):
                    @block.tensor
                    def _(e, tg=tg):
                        e.wait_ge(sV.h, sV.v)
                        for d in range(8):
                            mm = e.transpose(out=pb[:, d * 128:(d + 1) * 128],
                                             in_=eoutF[:, d * 512 + tg * 128: d * 512 + (tg + 1) * 128],
                                             identity=idbf[:, :])
                        sT.inc(mm)

                    @BV
                    def _(e, tg=tg, x1tok=x1tok):
                        e.wait_ge(sT.h, sT.v)
                        W(e, x1tok)
                        sV.inc(e.tensor_tensor(out=x1sl[:, tg * E:(tg + 1) * E],
                                               in0=x1sl[:, tg * E:(tg + 1) * E],
                                               in1=pb[:, 0:E], op=ALU.add))

                @BS
                def _(e, tok0=tok0):
                    e.wait_ge(sV.h, sV.v)
                    sO.inc(e.dma_start(
                        out=X1[tok0:tok0 + SLAB, :].rearrange("(a b) c -> b a c", b=128),
                        in_=x1sl[:, :].rearrange("p (a c) -> p a c", a=4)), 16)
                    e.wait_ge(sO.h, sO.v)

        barrier()
        if upto == "p4n":
            dbg_end()
            return nc
        if upto == "p4":
            dump_dram_to_Y(X1)
            return nc

        escope(_sc[0])
        _sc[0] = scope("P5a")
        with ExitStack() as p5:
            E5 = p5.enter_context
            xgF = E5(nc.sbuf_tensor("xgF", [128, KC * CAP], BF16))
            xgt = E5(nc.sbuf_tensor("xgt", [128, 4 * E], BF16))
            wgbuf = E5(nc.sbuf_tensor("wgbuf5", [128, 2 * E], BF16))
            wubuf = E5(nc.sbuf_tensor("wubuf5", [128, 2 * E], BF16))
            wtg5 = E5(nc.sbuf_tensor("wtg5", [128, 2 * E], BF16))
            wtu5 = E5(nc.sbuf_tensor("wtu5", [128, 2 * E], BF16))
            gtmp = E5(nc.sbuf_tensor("gtmp5", [128, 2560], BF16))
            hebuf = E5(nc.sbuf_tensor("hebuf", [128, 2 * CAP], BF16))

            xgm = []
            for j in range(20):
                gtok = [None]

                @block.gpsimd
                def _(e, j=j):
                    if j == 0:
                        e.wait_ge(sC.h, sC.v)
                        e.wait_ge(sV.h, sV.v)
                    if j >= 4:
                        e.wait_ge(sT.h, xgm[j - 4])
                    sm = rXG.s[j % 4]
                    sm.inc(e.indirect_dma_start(
                        xgt[:, (j % 4) * E:(j % 4) * E + E], None,
                        xf_all.ap(), IndirectOffsetOnAxis(dispi[:, j:j + 1], 0)), 16)
                    gtok[0] = sm.tok()

                @block.tensor
                def _(e, j=j):
                    W(e, gtok[0])
                    e.wait_ge(sV.h, sV.v)
                    for fc in range(KC):
                        mm = e.transpose(out=pb[:, fc * 128:(fc + 1) * 128],
                                         in_=xgt[:, (j % 4) * E + fc * 128:(j % 4) * E + (fc + 1) * 128],
                                         identity=idbf[:, :])
                    xgm.append(sT.inc(mm))

                @BV
                def _(e, j=j):
                    e.wait_ge(sT.h, sT.v)
                    sV.inc(e.tensor_copy(
                        out=xgF[:, :].rearrange("p (a b c) -> p a b c", a=KC, b=20)[:, :, j, :],
                        in_=pb[:, :].rearrange("p (a c) -> p a c", a=KC)))

            escope(_sc[0])
            _sc[0] = scope("P5au")
            hem = []
            tgm5, tum5, wcg5, wcu5, gum5 = [], [], [], [], []
            for m in range(32):
                ltok = [None]

                @block.gpsimd
                def _(e, m=m):
                    if m >= 2:
                        e.wait_ge(sT.h, tum5[m - 2])
                        e.wait_ge(sT.h, hem[m - 2][0])
                    elif wt_guard[0]:
                        e.wait_ge(sT.h, wt_guard[0])
                    rWT.dma(e, m % 2, wtg5[:, (m % 2) * E:(m % 2) * E + E],
                            ap["rgT"][m * 128:(m + 1) * 128, :])
                    ltok[0] = rWT.dma(e, m % 2, wtu5[:, (m % 2) * E:(m % 2) * E + E],
                                      ap["ruT"][m * 128:(m + 1) * 128, :])

                @block.tensor
                def _(e, m=m):
                    W(e, ltok[0])
                    if m == 0:
                        e.wait_ge(sV.h, sV.v)
                        e.wait_ge(sWC.h, sWC.v)
                    else:
                        e.wait_ge(sWC.h, wcg5[m - 1])
                    for k in range(KC):
                        mm = e.transpose(out=pb[:, k * 128:(k + 1) * 128],
                                         in_=wtg5[:, (m % 2) * E + k * 128:(m % 2) * E + (k + 1) * 128],
                                         identity=idbf[:, :])
                    tgm5.append(sT.inc(mm))
                    if m > 0:
                        e.wait_ge(sWC.h, wcu5[m - 1])
                    for k in range(KC):
                        mm = e.transpose(out=pb2[:, k * 128:(k + 1) * 128],
                                         in_=wtu5[:, (m % 2) * E + k * 128:(m % 2) * E + (k + 1) * 128],
                                         identity=idbf[:, :])
                    tum5.append(sT.inc(mm))

                @block.vector
                def _(e, m=m):
                    e.wait_ge(sT.h, tgm5[m])
                    if m >= 2:
                        e.wait_ge(sT.h, gum5[m - 2])
                    wcg5.append(sWC.inc(e.tensor_copy(
                        out=wgbuf[:, (m % 2) * E:(m % 2) * E + E], in_=pb[:, 0:E])))
                    e.wait_ge(sT.h, tum5[m])
                    wcu5.append(sWC.inc(e.tensor_copy(
                        out=wubuf[:, (m % 2) * E:(m % 2) * E + E], in_=pb2[:, 0:E])))

                gm5 = []
                um5 = []
                gacq = []
                uacq = []

                @block.tensor
                def _(e, m=m):
                    gacq.append(pool.acquire(e, 5))
                    e.wait_ge(sWC.h, wcg5[m])
                    for k in range(KC):
                        for sub in range(5):
                            sT.inc(e.matmul(
                                out=gacq[0][1][:, sub * 512:(sub + 1) * 512],
                                lhsT=wgbuf[:, (m % 2) * E + k * 128:(m % 2) * E + (k + 1) * 128],
                                rhs=fchunk(xgF, CAP)[:, k, sub * 512:(sub + 1) * 512],
                                start=(k == 0), stop=(k == KC - 1)))
                    gm5.append(sT.v)

                @BS
                def _(e, m=m):
                    e.wait_ge(sT.h, gm5[0])
                    v = sS.inc(e.activation(out=gtmp[:, :], in_=gacq[0][1][:, 0:CAP],
                                            func=AF.Silu))
                    pool.release(gacq[0][0], 5, sS.h, v)

                @block.tensor
                def _(e, m=m):
                    uacq.append(pool.acquire(e, 5))
                    e.wait_ge(sWC.h, wcu5[m])
                    for k in range(KC):
                        for sub in range(5):
                            sT.inc(e.matmul(
                                out=uacq[0][1][:, sub * 512:(sub + 1) * 512],
                                lhsT=wubuf[:, (m % 2) * E + k * 128:(m % 2) * E + (k + 1) * 128],
                                rhs=fchunk(xgF, CAP)[:, k, sub * 512:(sub + 1) * 512],
                                start=(k == 0), stop=(k == KC - 1)))
                    um5.append(sT.v)

                @BV
                def _(e, m=m):
                    e.wait_ge(sT.h, um5[0])
                    e.wait_ge(sS.h, sS.v)
                    if m >= 2:
                        W(e, hem[m - 2][1])
                    v = sV.inc(e.scalar_tensor_tensor(
                        out=hebuf[:, (m % 2) * CAP:(m % 2) * CAP + CAP],
                        in0=uacq[0][1][:, 0:CAP], scalar=1.0, in1=gtmp[:, :],
                        op0=ALU.mult, op1=ALU.mult))
                    pool.release(uacq[0][0], 5, sV.h, v)

                gum5.append(sT.v)
                vlast = sV.v

                @block.gpsimd
                def _(e, m=m, vlast=vlast):
                    e.wait_ge(sV.h, vlast)
                    hetok = rHE.dma(e, m % 2, HE[m * 128:(m + 1) * 128, :],
                                    hebuf[:, (m % 2) * CAP:(m % 2) * CAP + CAP])
                    hem.append((sT.v, hetok))

        barrier()
        if upto == "p5an":
            dbg_end()
            return nc

        escope(_sc[0])
        _sc[0] = scope("P5b")
        with ExitStack() as p5b:
            E5 = p5b.enter_context
            rdsb = E5(nc.sbuf_tensor("rdsb", [128, 32 * E], BF16))
            hein = E5(nc.sbuf_tensor("hein", [128, 4 * 512], BF16))
            eoutF = E5(nc.sbuf_tensor("eoutF5", [128, 8 * 512], BF16))
            erow = E5(nc.sbuf_tensor("erow", [128, E], BF16))

            @block.gpsimd
            def _(e):
                for m in range(32):
                    sRD.inc(e.dma_start(out=rdsb[:, m * E:(m + 1) * E],
                                        in_=ap["rd"][m * 128:(m + 1) * 128, :]), 16)
                sG.inc(e.dma_start(out=rout_in[CAP:CAP1, :], in_=zbf[:, :]), 16)
                e.wait_ge(sG.h, sG.v)
            rdtok = sRD.tok()

            ertoks = []
            hi_guard = [0]
            for sub in range(5):
                dbanks = []

                @block.tensor
                def _(e, rdtok=rdtok):
                    W(e, rdtok)
                    for d in range(6):
                        dbanks.append(pool.acquire(e, 1))

                dm = []
                for m in range(32):
                    htok = [None]

                    @block.sync
                    def _(e, m=m, sub=sub):
                        if m >= 4:
                            e.wait_ge(sT.h, dm[m - 4])
                        elif hi_guard[0]:
                            e.wait_ge(sT.h, hi_guard[0])
                        htok[0] = rHI.dma(e, m % 4,
                                          hein[:, (m % 4) * 512:(m % 4) * 512 + 512],
                                          HE[m * 128:(m + 1) * 128, sub * 512:(sub + 1) * 512])

                    @block.tensor
                    def _(e, m=m):
                        W(e, htok[0])
                        for d in range(6):
                            sT.inc(e.matmul(out=dbanks[d][1][:, 0:512],
                                            lhsT=rdsb[:, m * E + d * 128: m * E + (d + 1) * 128],
                                            rhs=hein[:, (m % 4) * 512:(m % 4) * 512 + 512],
                                            start=(m == 0), stop=(m == 31)))
                        dm.append(sT.v)
                hi_guard[0] = dm[-1]

                @BV
                def _(e):
                    e.wait_ge(sT.h, sT.v)
                    for d in range(6):
                        b0, pp = dbanks[d]
                        v = sV.inc(e.tensor_copy(out=eoutF[:, d * 512:(d + 1) * 512],
                                                 in_=pp[:, 0:512]))
                        pool.release(b0, 1, sV.h, v)

                d8b = []

                @block.tensor
                def _(e):
                    d8b.append(pool.acquire(e, 1))
                    d8b.append(pool.acquire(e, 1))

                dm2 = []
                for m in range(32):
                    htok = [None]

                    @block.sync
                    def _(e, m=m, sub=sub):
                        if m >= 4:
                            e.wait_ge(sT.h, dm2[m - 4])
                        elif hi_guard[0]:
                            e.wait_ge(sT.h, hi_guard[0])
                        htok[0] = rHI.dma(e, m % 4,
                                          hein[:, (m % 4) * 512:(m % 4) * 512 + 512],
                                          HE[m * 128:(m + 1) * 128, sub * 512:(sub + 1) * 512])

                    @block.tensor
                    def _(e, m=m):
                        W(e, htok[0])
                        for di in range(2):
                            sT.inc(e.matmul(out=d8b[di][1][:, 0:512],
                                            lhsT=rdsb[:, m * E + (6 + di) * 128: m * E + (7 + di) * 128],
                                            rhs=hein[:, (m % 4) * 512:(m % 4) * 512 + 512],
                                            start=(m == 0), stop=(m == 31)))
                        dm2.append(sT.v)
                hi_guard[0] = dm2[-1]

                @BV
                def _(e):
                    e.wait_ge(sT.h, sT.v)
                    for di in range(2):
                        b0, pp = d8b[di]
                        v = sV.inc(e.tensor_copy(out=eoutF[:, (6 + di) * 512:(7 + di) * 512],
                                                 in_=pp[:, 0:512]))
                        pool.release(b0, 1, sV.h, v)

                for tg in range(4):
                    @block.tensor
                    def _(e, tg=tg):
                        e.wait_ge(sV.h, sV.v)
                        for d in range(8):
                            mm = e.transpose(out=pb[:, d * 128:(d + 1) * 128],
                                             in_=eoutF[:, d * 512 + tg * 128: d * 512 + (tg + 1) * 128],
                                             identity=idbf[:, :])
                        sT.inc(mm)

                    @BV
                    def _(e, tg=tg):
                        e.wait_ge(sT.h, sT.v)
                        if ertoks:
                            W(e, ertoks[-1])
                        sV.inc(e.tensor_copy(out=erow[:, :], in_=pb[:, 0:E]))

                    @block.gpsimd
                    def _(e, sub=sub, tg=tg):
                        e.wait_ge(sV.h, sV.v)
                        ertoks.append(rER.dma(
                            e, 0,
                            rout_in[sub * 512 + tg * 128: sub * 512 + (tg + 1) * 128, :],
                            erow[:, :]))

            @block.gpsimd
            def _(e):
                W(e, ertoks[-1])
                sC.inc(e.collective_compute(
                    "AllGather", ALU.bypass, replica_groups=[list(range(NCORE))],
                    ins=[rout_in.ap().opt()], outs=[rout_all.ap().opt()]))

        barrier()
        if upto == "p5n":
            dbg_end()
            return nc
        if upto == "p5":
            dump_dram_to_Y(X1)
            return nc

        escope(_sc[0])
        _sc[0] = scope("P6")
        with ExitStack() as p6:
            E6 = p6.enter_context
            r1b = E6(nc.sbuf_tensor("r1b", [128, 2 * E], BF16))
            r2b = E6(nc.sbuf_tensor("r2b", [128, 2 * E], BF16))
            x1in = E6(nc.sbuf_tensor("x1in", [128, 2 * E], F32))
            outb = E6(nc.sbuf_tensor("outb", [128, 2 * E], F32))

            om = []
            stom = []
            for t in range(NT):
                rtok = [None]

                @block.gpsimd
                def _(e, t=t):
                    if t == 0:
                        e.wait_ge(sC.h, sC.v)
                    if t >= 2:
                        e.wait_ge(sV.h, om[t - 2])
                    sm = rRR.s[t % 2]
                    sm.inc(e.indirect_dma_start(
                        r1b[:, (t % 2) * E:(t % 2) * E + E], None,
                        rout_all.ap(), IndirectOffsetOnAxis(loca[:, t:t + 1], 0)), 16)
                    sm.inc(e.indirect_dma_start(
                        r2b[:, (t % 2) * E:(t % 2) * E + E], None,
                        rout_all.ap(), IndirectOffsetOnAxis(loca[:, 16 + t:16 + t + 1], 0)), 16)
                    rtok[0] = sm.tok()

                xtok = [None]

                @block.sync
                def _(e, t=t):
                    if t >= 2:
                        e.wait_ge(sV.h, om[t - 2])
                    xtok[0] = rXI.dma(e, t % 2,
                                      x1in[:, (t % 2) * E:(t % 2) * E + E],
                                      X1.rearrange("(a b) c -> a b c", b=128)[t])

                @BV
                def _(e, t=t):
                    W(e, rtok[0])
                    W(e, xtok[0])
                    if t >= 2:
                        e.wait_ge(sO.h, stom[t - 2])
                    e.scalar_tensor_tensor(out=outb[:, (t % 2) * E:(t % 2) * E + E],
                                           in0=r1b[:, (t % 2) * E:(t % 2) * E + E],
                                           scalar=st(W1, t, 1),
                                           in1=x1in[:, (t % 2) * E:(t % 2) * E + E],
                                           op0=ALU.mult, op1=ALU.add)
                    om.append(sV.inc(e.scalar_tensor_tensor(
                        out=outb[:, (t % 2) * E:(t % 2) * E + E],
                        in0=r2b[:, (t % 2) * E:(t % 2) * E + E],
                        scalar=st(W2, t, 1),
                        in1=outb[:, (t % 2) * E:(t % 2) * E + E],
                        op0=ALU.mult, op1=ALU.add)))

                @BS
                def _(e, t=t):
                    e.wait_ge(sV.h, om[t])
                    sO.inc(e.dma_start(out=Y.rearrange("(a b) c -> a b c", b=128)[t],
                                       in_=outb[:, (t % 2) * E:(t % 2) * E + E]), 16)
                    e.wait_ge(sO.h, sO.v)
                    stom.append(sO.v)

            @BS
            def _(e):
                e.wait_ge(sO.h, sO.v)

        escope(_sc[0])

    return nc


_NC_CACHE = None


def kernel(**inputs):
    global _NC_CACHE
    f32 = np.float32
    x = np.ascontiguousarray(np.asarray(inputs["L0_attn_x"], f32))
    gamma = np.asarray(inputs["L0_attn_gamma"], f32)
    beta = np.asarray(inputs["L0_attn_beta"], f32)
    shared = dict(
        Wq=np.ascontiguousarray(np.asarray(inputs["L0_attn_Wq"], f32)),
        Wk=np.ascontiguousarray(np.asarray(inputs["L0_attn_Wk"], f32)),
        Wv=np.ascontiguousarray(np.asarray(inputs["L0_attn_Wv"], f32)),
        Wo=np.ascontiguousarray(np.asarray(inputs["L0_attn_Wo"], f32)),
        Wg1=np.ascontiguousarray(np.asarray(inputs["L0_attn_Wg1"], f32)),
        Wg2=np.ascontiguousarray(np.asarray(inputs["L0_attn_Wg2"], f32)),
        gammaB=np.ascontiguousarray(np.broadcast_to(gamma[None, :], (128, E))),
        betaB=np.ascontiguousarray(np.broadcast_to(beta[None, :], (128, E))),
        rwT=np.ascontiguousarray(np.asarray(inputs["L0_ffn_router_w"], f32).T),
        sgT=np.ascontiguousarray(np.asarray(inputs["L0_ffn_sg"], f32).T),
        suT=np.ascontiguousarray(np.asarray(inputs["L0_ffn_su"], f32).T),
        sd=np.ascontiguousarray(np.asarray(inputs["L0_ffn_sd"], f32)),
    )
    if _NC_CACHE is None:
        _NC_CACHE = build_nc()
    nc = _NC_CACHE
    in_maps = []
    for c in range(NCORE):
        im = dict(shared)
        im.update({k: np.ascontiguousarray(v) for k, v in _consts(c).items()})
        im["X"] = np.ascontiguousarray(x[c])
        im["rgT"] = np.ascontiguousarray(np.asarray(inputs["L0_ffn_rg"][c], f32).T)
        im["ruT"] = np.ascontiguousarray(np.asarray(inputs["L0_ffn_ru"][c], f32).T)
        im["rd"] = np.ascontiguousarray(np.asarray(inputs["L0_ffn_rd"][c], f32))
        in_maps.append(im)
    res = run_bass_kernel_spmd(nc, in_maps, core_ids=list(range(NCORE)))
    return np.stack([res.results[c]["Y"] for c in range(NCORE)], axis=0)
